# version 05
# baseline (speedup 1.0000x reference)
import numpy as np
import concourse.bass as bass
import concourse.mybir as mybir
from concourse.bass_types import AP
from contextlib import ExitStack

F32 = mybir.dt.float32
BF16 = mybir.dt.bfloat16
AF = mybir.ActivationFunctionType
ALU = mybir.AluOpType

P = 128
C = 1024
HD = 4096
TLOC = 4096
NT = 32
KB = 8
HB = 32
NTOK = 512
T3 = 8
NST = 64
LN0_EPS = 1e-5
LN1_EPS = 1e-5
LNF_EPS = 1e-3
LN2_EPS = 1e-5
FFN_EPS = 1e-5
MAGIC = 12582912.0
SB_TOP = 229344


def build(debug=()):
    nc = bass.Bass("TRN2", target_bir_lowering=False, detect_race_conditions=False)
    dbg = set(debug)

    def din(name, shape, dt=F32):
        return nc.dram_tensor(name, shape, dt, kind="ExternalInput").ap()

    def dint(name, shape, dt=F32):
        return nc.dram_tensor(name, shape, dt, kind="Internal").ap()

    x_in = din("x_sh", [TLOC, C])
    xT_in = din("xT_sh", [C, TLOC])
    log_tau = din("log_tau", [NST, 1])
    log_dt = din("log_dt", [1, 1])
    mix_in = din("mix", [C, NST])
    d_in = din("dvec", [1, C])
    n1w_in = din("n1w", [1, C]); n1b_in = din("n1b", [1, C])
    n2w_in = din("n2w", [1, C]); n2b_in = din("n2b", [1, C])
    fc1w_in = din("fc1w", [HD, C]); fc1b_in = din("fc1b", [1, HD])
    fc2w_in = din("fc2w", [C, HD]); fc2b_in = din("fc2b", [1, C])
    ffnw_in = din("ffnw", [C, C])
    lng_in = din("lng", [1, C]); lnb_in = din("lnb", [1, C])
    emask_in = din("emask", [1, 1])
    omask_in = din("omask", [1, 1])
    iota128 = din("iota128", [1, P])
    iota128p1 = din("iota128p1", [1, P])
    rev127_in = din("rev127", [P, 1])
    j128_in = din("j128", [1, NT + 1])

    out_hbm = nc.dram_tensor("out_sh", [C, TLOC], F32, kind="ExternalOutput").ap()
    dbg_o = {}
    for name, shape, dt in [
        ("xn", [TLOC, C], F32), ("raw1", [TLOC, C], BF16), ("raw2", [TLOC, C], BF16),
        ("yi", [P, NT * C], BF16), ("xattn", [TLOC, C], F32), ("ln2", [TLOC, C], BF16),
        ("gsp", [C, TLOC], BF16), ("mlp", [C, TLOC], BF16), ("xn2", [KB, P * TLOC], BF16),
        ("wq", [P, KB * C], BF16), ("ksrc", [C, 256], BF16), ("bpre", [P, C], BF16),
        ("sm", [P, NT * 512], BF16), ("scal", [1, 64], F32),
    ]:
        if name in dbg:
            dbg_o[name] = nc.dram_tensor("dbg_" + name, shape, dt, kind="ExternalOutput").ap()

    xn_sp = dint("xn_sp", [TLOC, C])
    raw2_sp = dint("raw2_sp", [TLOC, C], BF16)
    xattn_sp = dint("xattn_sp", [TLOC, C])
    ln2_sp = dint("ln2_sp", [TLOC, C], BF16)
    ksrc_hbm = dint("ksrc_hbm", [C, 256], BF16)
    mix_bf = dint("mix_bf", [C, P], BF16)
    mixsm_hbm = dint("mixsm_hbm", [512, P], BF16)
    fc1w_bf = dint("fc1w_bf", [HD, C], BF16)
    fc2w_bf = dint("fc2w_bf", [C, HD], BF16)
    wq_hbm = dint("wq_hbm", [P, KB * C], BF16)
    xn2_sp = dint("xn2_sp", [KB, P * TLOC], BF16)
    g_sp = dint("g_sp", [C, TLOC], BF16)
    mlp_sp = dint("mlp_sp", [C, TLOC], BF16)
    bpre_cm = dint("bpre_cm", [C, P], BF16)
    scal_hbm = dint("scal_hbm", [1, 64])
    pbc_hbm = dint("pbc_hbm", [1, NST])
    cc_gin = dint("cc_gin", [1, P]); cc_gout = dint("cc_gout", [1, P])
    cc_sin = dint("cc_sin", [P, 512]); cc_sout = dint("cc_sout", [P, 512])

    SB_BASE = 18400

    def sb_at(name, shape, dt, offset):
        return nc.alloc_sbuf_tensor_at(name, shape, dt, offset=offset)

    def mk_region(lo, size):
        return [lo, lo + size]

    def alloc(region, name, shape, dt=F32):
        nbytes = int(np.prod(shape[1:])) * mybir.dt.size(dt)
        o = region[0]
        region[0] = (o + nbytes + 31) & ~31
        assert region[0] <= region[1], f"{name}: overflow {region}"
        return sb_at(name, shape, dt, o)

    RP = mk_region(SB_BASE, 20480)
    RS0 = RP[1]
    RS_SZ = 26624
    RC0 = RS0 + RS_SZ
    RC_SZ = 32768
    RA0 = RC0 + RC_SZ
    RB0 = RA0 + 65536
    assert RB0 + 65536 <= SB_TOP, (RB0 + 65536, SB_TOP)

    einT = alloc(RP, "einT", [P, NST], BF16)
    aoutD = alloc(RP, "aoutD", [P, P], BF16)
    mixT_sm = alloc(RP, "mixT_sm", [P, 512], BF16)
    mixpT_sm = alloc(RP, "mixpT_sm", [P, 512], BF16)
    bm1 = alloc(RP, "bm1", [P, 512], BF16)
    zqb = alloc(RP, "zqb", [P, 512], BF16)
    b1g = alloc(RP, "b1g", [P, 512], BF16)
    s_raw = alloc(RP, "s_raw", [P, 512], F32)
    s_tmp = alloc(RP, "s_tmp", [P, 512], F32)
    geoj = alloc(RP, "geoj", [P, NT + 1], F32)
    npj = alloc(RP, "npj", [P, NT + 1], F32)
    bpre = alloc(RP, "bpre", [P, C], BF16)
    w1_pb = alloc(RP, "w1_pb", [P, 32], F32)
    b1_pb = alloc(RP, "b1_pb", [P, 32], F32)
    dw1_pb = alloc(RP, "dw1_pb", [P, KB], F32)
    d_pb = alloc(RP, "d_pb", [P, 32], F32)
    n2w_pb = alloc(RP, "n2w_pb", [P, 32], F32)
    n2b_pb = alloc(RP, "n2b_pb", [P, 32], F32)
    g_pb = alloc(RP, "g_pb", [P, 32], F32)
    bta_pb = alloc(RP, "bta_pb", [P, 32], F32)
    fc1b_pb = alloc(RP, "fc1b_pb", [P, HB], F32)
    fc2b_pb = alloc(RP, "fc2b_pb", [P, 32], F32)
    b1p = alloc(RP, "b1p", [P, HB], F32)
    w1sm = alloc(RP, "w1sm", [P, 512], BF16)
    b1sm = alloc(RP, "b1sm", [P, 512], BF16)
    ltau = alloc(RP, "ltau", [P, 1], F32)
    negp = alloc(RP, "negp", [P, 1], F32)
    tauv = alloc(RP, "tauv", [P, 1], F32)
    dec = alloc(RP, "dec", [P, 1], F32)
    geoL = alloc(RP, "geoL", [P, 1], F32)
    dtbc = alloc(RP, "dtbc", [P, 1], F32)
    emask = alloc(RP, "emask", [P, 1], F32)
    omask = alloc(RP, "omask", [P, 1], F32)
    rqbc = alloc(RP, "rqbc", [P, 1], F32)
    ysc = alloc(RP, "ysc", [P, 1], F32)
    rev127 = alloc(RP, "rev127", [P, 1], F32)
    io2_bc = alloc(RP, "io2_bc", [P, P], F32)
    j128_bc = alloc(RP, "j128_bc", [P, NT + 1], F32)
    prow_bc = alloc(RP, "prow_bc", [P, NST], F32)
    smax = alloc(RP, "smax", [P, 1], F32)
    srow = alloc(RP, "srow", [1, P], F32)
    sc1 = alloc(RP, "sc1", [1, 16], F32)
    ajb = [alloc(RP, f"ajb{k}", [P, P], BF16) for k in range(2)]
    agjb = [alloc(RP, f"agjb{k}", [P, P], BF16) for k in range(2)]
    mv0 = alloc(RP, "mv0", [P, 2], F32)
    mv1 = alloc(RP, "mv1", [P, 2], F32)
    stA = alloc(RP, "stA", [P, 12], F32)
    r1 = alloc(RP, "r1", [P, 1], F32); r2 = alloc(RP, "r2", [P, 1], F32)
    r3 = alloc(RP, "r3", [P, 1], F32); r4 = alloc(RP, "r4", [P, 1], F32)
    r5 = alloc(RP, "r5", [P, 1], F32)
    rsg0 = alloc(RP, "rsg0", [P, 1], F32); nm0 = alloc(RP, "nm0", [P, 1], F32)
    rsg1 = alloc(RP, "rsg1", [P, 1], F32); nm1 = alloc(RP, "nm1", [P, 1], F32)
    rsg2 = alloc(RP, "rsg2", [P, 1], F32); nm2 = alloc(RP, "nm2", [P, 1], F32)
    rsg3 = alloc(RP, "rsg3", [P, 1], F32); nm3 = alloc(RP, "nm3", [P, 1], F32)
    rmx = alloc(RP, "rmx", [P, 1], F32)
    stA_g = alloc(RP, "stA_g", [P, 48], F32)
    mv0g = alloc(RP, "mv0g", [P, 8], F32)
    mv1g = alloc(RP, "mv1g", [P, 8], F32)
    rsg0g = alloc(RP, "rsg0g", [P, 4], F32)
    nm0g = alloc(RP, "nm0g", [P, 4], F32)
    rsg1g = alloc(RP, "rsg1g", [P, 4], F32)
    nm1g = alloc(RP, "nm1g", [P, 4], F32)
    rsg2g = alloc(RP, "rsg2g", [P, 4], F32)
    nm2g = alloc(RP, "nm2g", [P, 4], F32)
    rsg3g = alloc(RP, "rsg3g", [P, 4], F32)
    nm3g = alloc(RP, "nm3g", [P, 4], F32)
    rt4 = alloc(RP, "rt4", [P, 4], F32)
    vd = alloc(RP, "vd", [P, 1], F32)
    identf = alloc(RP, "identf", [P, P], F32)
    pcol1 = alloc(RP, "pcol1", [P, 1], F32)
    ceps5 = alloc(RP, "ceps5", [P, 1], F32)
    n2b_bf = alloc(RP, "n2b_bf", [P, KB], BF16)
    ceps3 = alloc(RP, "ceps3", [P, 1], F32)

    S0 = mk_region(RS0, RS_SZ)
    ks_st = alloc(S0, "ks_st", [P, 256], BF16)
    mixT = alloc(S0, "mixT", [P, C], BF16)
    io1_bc = alloc(S0, "io1_bc", [P, P], F32)
    ek32 = alloc(S0, "ek32", [P, P], F32)
    ekbf = alloc(S0, "ekbf", [P, P], BF16)
    cume = alloc(S0, "cume", [P, P], F32)
    cumebf = alloc(S0, "cumebf", [P, P], BF16)
    onesq = alloc(S0, "onesq", [P, P], F32)
    stgpb = alloc(S0, "stgpb", [32, P], F32)
    bpst = alloc(S0, "bpst", [P, KB * P], BF16)
    junk = alloc(S0, "junk", [P, C], F32)
    wqst = alloc(S0, "wqst", [P, C], BF16)
    abs8 = alloc(S0, "abs8", [P, KB], F32)

    S1 = mk_region(RS0, RS_SZ)
    xbuf = [alloc(S1, f"xbuf{k}", [P, C], F32) for k in range(4)]
    r2buf = [alloc(S1, f"r2buf{k}", [P, C], BF16) for k in range(4)]

    S2 = mk_region(RS0, RS_SZ)
    xnre = [alloc(S2, f"xnre{k}", [P, C], F32) for k in range(4)]
    ln2b = [alloc(S2, f"ln2b{k}", [P, C], BF16) for k in range(4)]

    S3a = mk_region(RS0, RS_SZ)
    xqT = alloc(S3a, "xqT", [P, KB * NTOK], BF16)
    gst = [alloc(S3a, f"gst{k}", [P, NTOK], BF16) for k in range(2)]

    S3b = mk_region(RS0, RS_SZ)
    ln2T = alloc(S3b, "ln2T", [P, KB * NTOK], BF16)
    stg = [alloc(S3b, f"stg{k}", [P, NTOK], F32) for k in range(2)]
    xaT = [alloc(S3b, f"xaT{k}", [P, NTOK], F32) for k in range(2)]
    ostT = [alloc(S3b, f"ostT{k}", [P, NTOK], F32) for k in range(2)]
    gld = [alloc(S3b, f"gld{k}", [P, NTOK], BF16) for k in range(2)]
    xTld = [alloc(S3b, f"xTld{k}", [P, NTOK], F32) for k in range(2)]

    C1 = mk_region(RC0, RC_SZ)
    smbig = alloc(C1, "smbig", [P, NT * 512], BF16)
    C2 = mk_region(RC0, RC_SZ)
    hank = [alloc(C2, f"hank{k}", [P, 32 * P], BF16) for k in range(2)]
    wtoep = [alloc(C2, f"wtoep{k}", [P, 32 * P], BF16) for k in range(2)]
    C3 = mk_region(RC0, RC_SZ)
    gbig = alloc(C3, "gbig", [P, HB * NTOK], BF16)
    C4 = mk_region(RC0, RC_SZ)
    wq_sb = alloc(C4, "wq_sb", [P, KB * C], BF16)
    r2T = [alloc(C4, f"r2T{k}", [P, KB * NTOK], BF16) for k in range(2)]

    A1 = mk_region(RA0, 65536)
    rawbig = alloc(A1, "rawbig", [P, NT * C], BF16)
    A2 = mk_region(RA0, 65536)
    fc1wT = alloc(A2, "fc1wT", [P, KB * HD], BF16)
    A3 = mk_region(RA0, 65536)
    ffst = alloc(A3, "ffst", [P, KB * C], F32)

    B1 = mk_region(RB0, 65536)
    yibig = alloc(B1, "yibig", [P, NT * C], BF16)
    B2 = mk_region(RB0, 65536)
    fc2wT = alloc(B2, "fc2wT", [P, HB * C], BF16)

    ctx = ExitStack()
    psA = ctx.enter_context(nc.psum_tensor("psA", [P, 512], F32))
    psB = ctx.enter_context(nc.psum_tensor("psB", [P, 512], F32))
    psc = [ctx.enter_context(nc.psum_tensor(f"psc{k}", [P, 1024], F32)) for k in range(2)]
    pst = [ctx.enter_context(nc.psum_tensor(f"pst{k}", [P, 512], F32)) for k in range(2)]

    class Sem:
        def __init__(self, name):
            self.h = nc.alloc_semaphore(name)
            self.v = 0
        def inc(self, instr, k=1):
            instr.then_inc(self.h, k)
            self.v += k
            return self.v
    SV = Sem("sv"); SS = Sem("ss"); SG = Sem("sg"); ST = Sem("stm"); CCS = Sem("ccs")
    qCAST = Sem("qcast"); qWSM = Sem("qwsm"); qP0 = Sem("qp0"); qPB = Sem("qpb")
    qFST = Sem("qfst"); qSH = Sem("qsh"); qKS = Sem("qks"); qBP = Sem("qbp")
    qWQ = Sem("qwq"); qMXT = Sem("qmxt"); qBPT = Sem("qbpt"); qWQL = Sem("qwql")
    qLD = [Sem(f"qld{k}") for k in range(4)]
    qSPX = [Sem(f"qspx{k}") for k in range(4)]
    qSPR = [Sem(f"qspr{k}") for k in range(4)]
    qLD2 = [Sem(f"qld2{k}") for k in range(4)]
    qSPA = [Sem(f"qspa{k}") for k in range(4)]
    qSPL = [Sem(f"qspl{k}") for k in range(4)]
    qHK = [Sem(f"qhk{k}") for k in range(2)]
    qW1T = Sem("qw1t"); qW2T = Sem("qw2t")
    qR2T = [Sem(f"qr2t{k}") for k in range(2)]
    qX2S = Sem("qx2s")
    qX2L = [Sem(f"qx2l{k}") for k in range(2)]
    qG = [Sem(f"qg{k}") for k in range(2)]
    qM = [Sem(f"qm{k}") for k in range(2)]
    qL2T = Sem("ql2t"); qMT = Sem("qmt"); qGT = Sem("qgt")
    qXA = Sem("qxa"); qXR = Sem("qxr"); qOUT = Sem("qout"); qDBG = Sem("qdbg")
    ALLQ = [qCAST, qWSM, qP0, qPB, qFST, qSH, qKS, qBP, qWQ, qMXT, qBPT, qWQL,
            qW1T, qW2T, qX2S, qL2T, qMT, qGT, qXA, qXR, qOUT, qDBG] + \
        qLD + qSPX + qSPR + qLD2 + qSPA + qSPL + qHK + qR2T + qX2L + qG + qM

    block = ctx.enter_context(nc.Block())
    sy, sc, ve, te, gp = nc.sync, nc.scalar, nc.vector, nc.tensor, nc.gpsimd

    def vfence():
        return SV.inc(ve.tensor_copy(vd[:], ceps5[:]))
    ve.memset(ceps5[:], LN0_EPS)
    ve.memset(s_raw[:], 0.0)
    ve.memset(smax[:], 0.0)
    ve.memset(geoj[:, 0:1], 0.0)
    sv_ceps = nc.alloc_semaphore("ceps")
    ve.memset(ceps3[:], LNF_EPS).then_inc(sv_ceps, 1)
    sc.wait_ge(sv_ceps, 1)

    for i in range(4):
        qCAST.inc(gp.dma_start(out=fc1w_bf[i * 1024:(i + 1) * 1024, :], in_=fc1w_in[i * 1024:(i + 1) * 1024, :]), 16)
    for i in range(4):
        qCAST.inc(gp.dma_start(out=fc2w_bf[i * 256:(i + 1) * 256, :], in_=fc2w_in[i * 256:(i + 1) * 256, :]), 16)
    qCAST.inc(gp.dma_start(out=mix_bf[:, 0:NST], in_=mix_in), 16)
    qCAST.inc(gp.dma_start(out=AP(mixsm_hbm.tensor, 0, [[P, 512], [1, NST]]), in_=mix_in[0:512, :]), 16)
    qCAST.inc(gp.dma_start(out=AP(mixsm_hbm.tensor, NST, [[P, 512], [1, NST]]), in_=mix_in[512:C, :]), 16)
    gq_casts = qCAST.v

    qP0.inc(sy.dma_start(out=ltau[0:NST, :], in_=log_tau), 16)
    qP0.inc(sy.dma_start(out=ltau[NST:P, :], in_=log_tau), 16)
    qP0.inc(sy.dma_start(out=r1[0:1, :], in_=log_dt), 16)
    qP0.inc(sy.dma_start(out=emask[:], in_=AP(emask_in.tensor, 0, [[0, P], [1, 1]])), 16)
    qP0.inc(sy.dma_start(out=omask[:], in_=AP(omask_in.tensor, 0, [[0, P], [1, 1]])), 16)
    qP0.inc(sy.dma_start(out=io1_bc[:], in_=AP(iota128.tensor, 0, [[0, P], [1, P]])), 16)
    qP0.inc(sy.dma_start(out=io2_bc[:], in_=AP(iota128p1.tensor, 0, [[0, P], [1, P]])), 16)
    qP0.inc(sy.dma_start(out=j128_bc[:], in_=AP(j128_in.tensor, 0, [[0, P], [1, NT + 1]])), 16)
    qP0.inc(sy.dma_start(out=rev127[:], in_=rev127_in), 16)
    qP0.inc(sy.dma_start(out=pcol1[:], in_=AP(iota128p1.tensor, 0, [[1, P], [1, 1]])), 16)
    pb_jobs = [(w1_pb, n1w_in, KB), (b1_pb, n1b_in, KB), (n2w_pb, n2w_in, KB),
               (n2b_pb, n2b_in, KB), (d_pb, d_in, KB), (g_pb, lng_in, KB),
               (bta_pb, lnb_in, KB), (fc2b_pb, fc2b_in, KB), (fc1b_pb, fc1b_in, HB)]
    sv_pb = SV.v
    for dst, src, nb in pb_jobs:

        sq_st = qPB.inc(sy.dma_start(out=stgpb[0:nb, :], in_=AP(src.tensor, 0, [[P, nb], [1, P]])), 16)
        ve.wait_ge(qPB.h, sq_st)
        for jb in range(4):
            sv_pb = SV.inc(ve.transpose(dst[32 * jb:32 * (jb + 1), 0:32], stgpb[0:32, 32 * jb:32 * (jb + 1)]))
        sy.wait_ge(SV.h, sv_pb)
    qWSM.inc(gp.dma_start(out=w1sm[0:NST, :], in_=AP(n1w_in.tensor, 0, [[0, NST], [1, 512]])), 16)
    qWSM.inc(gp.dma_start(out=w1sm[NST:P, :], in_=AP(n1w_in.tensor, 512, [[0, NST], [1, 512]])), 16)
    qWSM.inc(gp.dma_start(out=b1sm[0:NST, :], in_=AP(n1b_in.tensor, 0, [[0, NST], [1, 512]])), 16)
    gq_sm = qWSM.inc(gp.dma_start(out=b1sm[NST:P, :], in_=AP(n1b_in.tensor, 512, [[0, NST], [1, 512]])), 16)
    for i in range(KB):
        qFST.inc(sy.dma_start(out=ffst[:, i * C:(i + 1) * C], in_=ffnw_in[i * P:(i + 1) * P, :]), 16)
    sq_p0 = qP0.v; sq_fst = qFST.v

    sc.wait_ge(qP0.h, sq_p0)
    ve.wait_ge(qP0.h, sq_p0)
    SV.inc(ve.tensor_scalar(identf[:], io2_bc[:], pcol1[:, 0:1], None, ALU.is_equal))
    sc.activation(r2[:], ltau[:], AF.Abs)
    sc.activation(r3[:], r2[:], AF.Exp, scale=-1.0)
    sc.activation(r4[:], r3[:], AF.Ln, bias=1.0)
    ss_sp1 = SS.inc(sc.activation(r5[:], ltau[:], AF.Relu))
    ve.wait_ge(SS.h, ss_sp1)
    sv_tau = SV.inc(ve.tensor_add(tauv[:], r4[:], r5[:]))
    sc.wait_ge(SV.h, sv_tau)
    sc.activation(r2[0:1, :], r1[0:1, :], AF.Abs)
    sc.activation(r3[0:1, :], r2[0:1, :], AF.Exp, scale=-1.0)
    sc.activation(r4[0:1, :], r3[0:1, :], AF.Ln, bias=1.0)
    ss_sp2 = SS.inc(sc.activation(r5[0:1, :], r1[0:1, :], AF.Relu))
    ve.wait_ge(SS.h, ss_sp2)
    SV.inc(ve.tensor_add(r1[0:1, :], r4[0:1, :], r5[0:1, :]))
    sv_dt = vfence()
    sy.wait_ge(SV.h, sv_dt)
    sq_dtw = qSH.inc(sy.dma_start(out=scal_hbm[:, 0:1], in_=r1[0:1, :]), 16)
    sy.wait_ge(qSH.h, sq_dtw)
    sq_dtr = qSH.inc(sy.dma_start(out=dtbc[:], in_=AP(scal_hbm.tensor, 0, [[0, P], [1, 1]])), 16)
    ve.wait_ge(qSH.h, sq_dtr)
    ve.wait_ge(SV.h, sv_tau)
    SV.inc(ve.tensor_scalar(negp[:], tauv[:], dtbc[:], -1.0, ALU.mult, ALU.mult))
    sv_negp = vfence()
    sy.wait_ge(SV.h, sv_negp)
    sq_pw = qSH.inc(sy.dma_start(out=pbc_hbm[:, :], in_=negp[0:NST, :]), 16)
    sy.wait_ge(qSH.h, sq_pw)
    sq_pr = qSH.inc(sy.dma_start(out=prow_bc[:], in_=AP(pbc_hbm.tensor, 0, [[0, P], [1, NST]])), 16)

    sc.wait_ge(qSH.h, sq_pr)
    sc.activation(einT[:], prow_bc[:], AF.Exp, scale=rev127[:])
    sc.activation(aoutD[:], io2_bc[:], AF.Exp, scale=negp[:])
    sc.activation(ek32[:], io1_bc[:], AF.Exp, scale=negp[:])
    sc.activation(ekbf[:], io1_bc[:], AF.Exp, scale=negp[:])
    sc.activation(dec[:], negp[:], AF.Exp, scale=128.0)
    ss_tabs = SS.inc(sc.activation(npj[:], j128_bc[:], AF.Identity, scale=negp[:]))

    ve.wait_ge(SS.h, ss_tabs)
    ve.memset(onesq[:], 1.0)
    sv_scan = SV.inc(ve.tensor_tensor_scan(cume[:], onesq[:], ek32[:], 0.0, ALU.mult, ALU.add))
    ve.wait_ge(SV.h, sv_scan)
    sv_cume = SV.inc(ve.tensor_copy(cumebf[:], cume[:]))
    SV.inc(ve.reduce_sum(geoL[:], ek32[:], axis=mybir.AxisListType.X))
    sv_geo0 = SV.v
    sc.wait_ge(SV.h, sv_geo0)
    for j in range(NT):
        ssj = SS.inc(sc.activation(r2[:], npj[:, j:j + 1], AF.Exp))
        ve.wait_ge(SS.h, ssj)
        svj = SV.inc(ve.scalar_tensor_tensor(geoj[:, j + 1:j + 2], r2[:], geoL[:], geoj[:, j:j + 1], ALU.mult, ALU.add))
        if j < NT - 1:
            sc.wait_ge(SV.h, svj)
    sv_geoj = SV.v

    sc.wait_ge(qCAST.h, gq_casts)
    qMXT.inc(sc.dma_start(out=mixT[:], in_=mix_bf, transpose=True), 16)
    aq_mixT = qMXT.inc(sc.dma_start(out=mixT_sm[:], in_=mixsm_hbm, transpose=True), 16)

    ve.wait_ge(qMXT.h, aq_mixT)
    ve.wait_ge(qWSM.h, gq_sm)
    ve.tensor_mul(mixpT_sm[:], mixT_sm[:], w1sm[:])
    ve.tensor_mul(bm1[:], mixT_sm[:], b1sm[:])
    ve.tensor_scalar(b1g[:], b1sm[:], geoj[:, NT:NT + 1], None, ALU.mult)
    sv_mixp = SV.inc(ve.tensor_mul(dw1_pb[:], d_pb[:, 0:KB], w1_pb[:, 0:KB]))

    te.wait_ge(qMXT.h, aq_mixT)
    te.wait_ge(SV.h, sv_cume)
    sv_evac = SV.v
    ve.wait_ge(SV.h, sv_mixp)
    ve.memset(ks_st[:], 0.0)
    sq_ks = []
    for cb in range(KB):
        if cb > 0:
            te.wait_ge(SV.h, sv_evac)
        te.matmul(pst[0][:, 0:P], mixT[0:NST, cb * P:(cb + 1) * P], ekbf[0:NST, :], start=True, stop=True)
        st_kl = ST.inc(te.matmul(pst[1][:, 0:P], mixT[0:NST, cb * P:(cb + 1) * P], cumebf[0:NST, :], start=True, stop=True))
        ve.wait_ge(ST.h, st_kl)
        if cb > 0:
            ve.wait_ge(qKS.h, sq_ks[-1])
        rev_dst = AP(ks_st, 127, [[256, P], [-1, P]])
        ve.tensor_scalar(rev_dst, pst[0][:, 0:P], w1_pb[:, cb:cb + 1], None, ALU.mult)
        ve.tensor_scalar(ks_st[:, 127:128], ks_st[:, 127:128], dw1_pb[:, cb:cb + 1], None, ALU.add)
        SV.inc(ve.tensor_scalar(bpst[:, cb * P:(cb + 1) * P], pst[1][:, 0:P], d_pb[:, cb:cb + 1], b1_pb[:, cb:cb + 1], ALU.add, ALU.mult))
        sv_evac = vfence()
        sy.wait_ge(SV.h, sv_evac)
        sq_ks.append(qKS.inc(sy.dma_start(out=ksrc_hbm[cb * P:(cb + 1) * P, :], in_=ks_st[:]), 16))
    sq_bpw = qBP.inc(sy.dma_start(
        out=AP(bpre_cm.tensor, 0, [[P, P], [P * P, KB], [1, P]]),
        in_=AP(bpst, 0, [[KB * P, P], [P, KB], [1, P]])), 16)
    sc.wait_ge(qBP.h, sq_bpw)
    aq_bpre = qBPT.inc(sc.dma_start(out=bpre[:], in_=bpre_cm, transpose=True), 16)

    sc.wait_ge(qFST.h, sq_fst)
    for cb in range(KB):
        ssab = SS.inc(sc.activation(junk[:], ffst[:, cb * C:(cb + 1) * C], AF.Abs, accum_out=abs8[:, cb:cb + 1]))
    ss_abs = SS.v
    ve.wait_ge(SS.h, ss_abs)
    SV.inc(ve.reduce_sum(r2[:], abs8[:], axis=mybir.AxisListType.X))
    sv_abs = vfence()
    sy.wait_ge(SV.h, sv_abs)
    sq_abw = qSH.inc(sy.dma_start(out=cc_gin[:, :], in_=r2[:]), 16)
    sy.wait_ge(qSH.h, sq_abw)
    sq_abr = qSH.inc(sy.dma_start(out=srow[:], in_=cc_gin), 16)
    ve.wait_ge(qSH.h, sq_abr)
    svt = SV.inc(ve.reduce_sum(sc1[:, 0:1], srow[:], axis=mybir.AxisListType.X))
    ve.wait_ge(SV.h, svt)
    svt = SV.inc(ve.tensor_scalar(sc1[:, 7:8], sc1[:, 0:1], 1.0 / (C * C), None, ALU.mult))
    ve.wait_ge(SV.h, svt)
    svt = SV.inc(ve.tensor_scalar(sc1[:, 1:2], sc1[:, 7:8], 1.0, FFN_EPS, ALU.mult, ALU.add))
    ve.wait_ge(SV.h, svt)
    SV.inc(ve.reciprocal(sc1[:, 2:3], sc1[:, 1:2]))
    sv_beta = vfence()
    sy.wait_ge(SV.h, sv_beta)
    sq_bw = qSH.inc(sy.dma_start(out=scal_hbm[:, 1:3], in_=sc1[:, 1:3]), 16)
    sy.wait_ge(qSH.h, sq_bw)
    sq_br = qSH.inc(sy.dma_start(out=rmx[:], in_=AP(scal_hbm.tensor, 2, [[0, P], [1, 1]])), 16)
    ve.wait_ge(qSH.h, sq_br)
    sq_wq = qWQ.v
    for cb in range(KB):
        if cb > 0:
            ve.wait_ge(qWQ.h, sq_wq)
        ve.tensor_scalar(junk[:], ffst[:, cb * C:(cb + 1) * C], rmx[:], MAGIC, ALU.mult, ALU.add)
        ve.tensor_scalar(junk[:], junk[:], MAGIC, 1.0, ALU.subtract, ALU.min)
        SV.inc(ve.tensor_scalar(wqst[:], junk[:], -1.0, None, ALU.max))
        sv_w = vfence()
        sy.wait_ge(SV.h, sv_w)
        sq_wq = qWQ.inc(sy.dma_start(out=wq_hbm[:, cb * C:(cb + 1) * C], in_=wqst[:]), 16)
    sv_p0_v = SV.v; ss_p0_s = SS.v; st_p0 = ST.v
    if "ksrc" in dbg_o:
        sy.wait_ge(qKS.h, sq_ks[-1])
        qDBG.inc(sy.dma_start(out=dbg_o["ksrc"], in_=ksrc_hbm), 16)
    if "wq" in dbg_o:
        sy.wait_ge(qWQ.h, sq_wq)
        qDBG.inc(sy.dma_start(out=dbg_o["wq"], in_=wq_hbm), 16)
    if "bpre" in dbg_o:
        sy.wait_ge(qBPT.h, aq_bpre)
        qDBG.inc(sy.dma_start(out=dbg_o["bpre"], in_=bpre[:]), 16)

    sy.wait_ge(SV.h, sv_p0_v)
    sy.wait_ge(SS.h, ss_p0_s)
    sy.wait_ge(qWQ.h, sq_wq)
    sy.wait_ge(qBPT.h, aq_bpre)
    sy.wait_ge(qKS.h, qKS.v)
    sy.wait_ge(qBP.h, qBP.v)
    sy.wait_ge(qPB.h, qPB.v)
    sy.wait_ge(ST.h, ST.v)
    sq_x = {}; ss_xn = {}; sv_r2 = {}; sq_xn = {}; sq_r2 = {}
    GW = 4

    def rsig_group(mvg, eps_ap, rsgg, nmg):
        var4 = AP(mvg, 1, [[8, P], [2, GW]])
        mu4 = AP(mvg, 0, [[8, P], [2, GW]])
        ss0 = SS.inc(sc.activation(rt4[:, 0:GW], var4, AF.Ln, bias=eps_ap))
        sc.wait_ge(SS.h, ss0)
        ss1 = SS.inc(sc.activation(rsgg[:, 0:GW], rt4[:, 0:GW], AF.Exp, scale=-0.5))
        ve.wait_ge(SS.h, ss1)
        return SV.inc(ve.scalar_tensor_tensor(nmg[:, 0:GW], mu4, -1.0, rsgg[:, 0:GW], ALU.mult, ALU.mult))

    for g in range(NT // GW):
        tiles = range(g * GW, (g + 1) * GW)
        for i in tiles:
            sl = i % GW
            if i >= GW:
                sy.wait_ge(qSPX[sl].h, qSPX[sl].v)
                sy.wait_ge(SV.h, sv_r2[i - GW])
            sq_x[i] = qLD[sl].inc(sy.dma_start(out=xbuf[sl][:], in_=x_in[i * P:(i + 1) * P, :]), 16)
        for i in tiles:
            sl = i % GW
            ve.wait_ge(qLD[sl].h, sq_x[i])
            SV.inc(ve.bn_stats(stA_g[:, sl * 12:sl * 12 + 6], xbuf[sl][:, 0:512]))
            svb = SV.inc(ve.bn_stats(stA_g[:, sl * 12 + 6:sl * 12 + 12], xbuf[sl][:, 512:1024]))
            ve.wait_ge(SV.h, svb)
            sq_x[(i, "a0")] = SV.inc(ve.bn_aggr(mv0g[:, 2 * sl:2 * sl + 2], stA_g[:, sl * 12:sl * 12 + 12]))
        sc.wait_ge(SV.h, sq_x[(tiles[-1], "a0")])
        sv_nm0 = rsig_group(mv0g, ceps5[:], rsg0g, nm0g)
        sc.wait_ge(SV.h, sv_nm0)
        for i in tiles:
            sl = i % GW
            ss_xn[i] = SS.inc(sc.activation(xbuf[sl][:], xbuf[sl][:], AF.Identity,
                                            scale=rsg0g[:, sl:sl + 1], bias=nm0g[:, sl:sl + 1]))
        for i in tiles:
            sl = i % GW
            ve.wait_ge(SS.h, ss_xn[i])
            SV.inc(ve.bn_stats(stA_g[:, sl * 12:sl * 12 + 6], xbuf[sl][:, 0:512]))
            svb = SV.inc(ve.bn_stats(stA_g[:, sl * 12 + 6:sl * 12 + 12], xbuf[sl][:, 512:1024]))
            ve.wait_ge(SV.h, svb)
            sq_x[(i, "a1")] = SV.inc(ve.bn_aggr(mv1g[:, 2 * sl:2 * sl + 2], stA_g[:, sl * 12:sl * 12 + 12]))
        sc.wait_ge(SV.h, sq_x[(tiles[-1], "a1")])
        rsig_group(mv1g, ceps5[:], rsg1g, nm1g)
        sv_nm2 = rsig_group(mv1g, ceps3[:], rsg2g, nm2g)
        ve.wait_ge(SV.h, sv_nm2)
        for i in tiles:
            sl = i % GW
            if i >= GW:
                ve.wait_ge(qSPR[sl].h, qSPR[sl].v)
            ve.tensor_scalar(rawbig[:, i * C:(i + 1) * C], xbuf[sl][:], rsg1g[:, sl:sl + 1], nm1g[:, sl:sl + 1], ALU.mult, ALU.add)
            sv_r2[i] = SV.inc(ve.tensor_scalar(r2buf[sl][:], xbuf[sl][:], rsg2g[:, sl:sl + 1], nm2g[:, sl:sl + 1], ALU.mult, ALU.add))
        svf = vfence()
        for i in tiles:
            sv_r2[i] = svf
        for i in tiles:
            sl = i % GW
            sy.wait_ge(SV.h, sv_r2[i])
            sq_xn[i] = qSPX[sl].inc(sy.dma_start(out=xn_sp[i * P:(i + 1) * P, :], in_=xbuf[sl][:]), 16)
            sq_r2[i] = qSPR[sl].inc(sy.dma_start(out=raw2_sp[i * P:(i + 1) * P, :], in_=r2buf[sl][:]), 16)

    if "xn" in dbg_o:
        for k in range(4):
            sy.wait_ge(qSPX[k].h, qSPX[k].v)
        qDBG.inc(sy.dma_start(out=dbg_o["xn"], in_=xn_sp), 16)
    if "raw2" in dbg_o:
        for k in range(4):
            sy.wait_ge(qSPR[k].h, qSPR[k].v)
        qDBG.inc(sy.dma_start(out=dbg_o["raw2"], in_=raw2_sp), 16)

    st_agg = {}; sv_s = {}
    for j in range(NT):
        te.wait_ge(SV.h, sv_r2[j])
        if j >= 2:
            te.wait_ge(SV.h, sv_s[j - 2])
        pa = psA if j % 2 == 0 else psB
        rb = rawbig[:, j * C:(j + 1) * C]
        te.matmul(pa[0:NST, 0:512], einT[:], rb[:, 0:512], start=True, stop=True)
        st_agg[j] = ST.inc(te.matmul(pa[NST:P, 0:512], einT[:], rb[:, 512:1024], start=True, stop=True, tile_position=(0, NST)))
        ve.wait_ge(ST.h, st_agg[j])
        ve.tensor_mul(smbig[:, j * 512:(j + 1) * 512], s_raw[:], mixpT_sm[:])
        sv_s[j] = SV.inc(ve.scalar_tensor_tensor(s_raw[:], s_raw[:], dec[:], pa[:, 0:512], ALU.mult, ALU.add))
    if "sm" in dbg_o:
        sy.wait_ge(SV.h, sv_s[NT - 1])
        qDBG.inc(sy.dma_start(out=dbg_o["sm"], in_=smbig[:]), 16)

    ve.tensor_mul(s_tmp[:], s_raw[:], w1sm[:])
    ve.tensor_add(s_tmp[:], s_tmp[:], b1g[:])
    SV.inc(ve.tensor_scalar(s_tmp[:], s_tmp[:], emask[:], None, ALU.mult))
    sv_send = vfence()
    sy.wait_ge(SV.h, sv_send)
    sq_ccw = qSH.inc(sy.dma_start(out=cc_sin, in_=s_tmp[:]), 16)
    gp.wait_ge(qSH.h, sq_ccw)
    cc1 = CCS.inc(gp.collective_compute(
        "AllReduce", ALU.add, replica_groups=[[0, 1], [2, 3], [4, 5], [6, 7]],
        ins=[cc_sin], outs=[cc_sout]))
    sy.wait_ge(CCS.h, cc1)
    sq_ccr = qSH.inc(sy.dma_start(out=s_tmp[:], in_=cc_sout), 16)
    ve.wait_ge(qSH.h, sq_ccr)
    ve.tensor_scalar(s_tmp[:], s_tmp[:], omask[:], None, ALU.mult)
    SV.inc(ve.tensor_mul(zqb[:], s_tmp[:], mixT_sm[:]))
    sv_zqb = vfence()

    te.wait_ge(SV.h, sv_zqb)
    te.wait_ge(qBPT.h, aq_bpre)
    sv_yi = {}
    ss_aj = {}; sv_agj = {}
    for j in range(NT):
        pb2 = j % 2
        sc.wait_ge(SV.h, sv_zqb if j < 2 else sv_yi[j - 2])
        ss_aj[j] = SS.inc(sc.activation(ajb[pb2][:], io2_bc[:], AF.Exp, scale=negp[:], bias=npj[:, j:j + 1]))
        ve.wait_ge(SS.h, ss_aj[j])
        SV.inc(ve.tensor_scalar(agjb[pb2][:], aoutD[:], geoj[:, j:j + 1], None, ALU.mult))
        sv_agj[j] = vfence()
        te.wait_ge(SV.h, sv_agj[j])
        if j >= 2:
            te.wait_ge(SV.h, sv_yi[j - 2])
        pc = psc[pb2]
        smj = smbig[:, j * 512:(j + 1) * 512]
        te.matmul(pc[:, 0:512], aoutD[0:NST, :], smj[0:NST, :], start=True, stop=False)
        te.matmul(pc[:, 0:512], agjb[pb2][0:NST, :], bm1[0:NST, :], start=False, stop=False)
        te.matmul(pc[:, 0:512], ajb[pb2][0:NST, :], zqb[0:NST, :], start=False, stop=True)
        te.matmul(pc[:, 512:1024], aoutD[NST:P, :], smj[NST:P, :], start=True, stop=False, tile_position=(NST, 0))
        te.matmul(pc[:, 512:1024], agjb[pb2][NST:P, :], bm1[NST:P, :], start=False, stop=False, tile_position=(NST, 0))
        st_c = ST.inc(te.matmul(pc[:, 512:1024], ajb[pb2][NST:P, :], zqb[NST:P, :], start=False, stop=True, tile_position=(NST, 0)))
        ve.wait_ge(ST.h, st_c)
        yi_dst = AP(yibig, j, [[NT * C, P], [32, C]])
        sv_yi[j] = SV.inc(ve.tensor_tensor(yi_dst, pc[:], bpre[:], ALU.add))
    sv_yi_all = SV.v

    NGRP = 32
    gq_h = {}; sv_rev = {}; st_tp = {}; sv_tev = {}
    gp.wait_ge(qKS.h, sq_ks[-1])
    gp.wait_ge(ST.h, ST.v)
    gp.wait_ge(SV.h, sv_yi_all)
    if "sm" in dbg_o:
        gp.wait_ge(qDBG.h, qDBG.v)
    for g in range(NGRP):
        hb2 = g % 2
        if g >= 2:
            gp.wait_ge(SV.h, sv_rev[g - 2])
        src = AP(ksrc_hbm.tensor, g * 32 * 256, [[1, P], [256, 32], [1, P]])
        gq_h[g] = qHK[hb2].inc(gp.dma_start(out=hank[hb2][:].rearrange("p (c j) -> p c j", c=32), in_=src), 16)
        ve.wait_ge(qHK[hb2].h, gq_h[g])
        if g >= 2:
            ve.wait_ge(ST.h, st_tp[g - 2])
        rev_src = AP(hank[hb2], 127, [[32 * P, P], [P, 32], [-1, P]])
        SV.inc(ve.tensor_copy(wtoep[hb2][:].rearrange("p (c j) -> p c j", c=32), rev_src))
        sv_rev[g] = vfence()
        te.wait_ge(SV.h, sv_rev[g])
        if g >= 2:
            te.wait_ge(SV.h, sv_tev[g - 2])
        tp_pair = (pst[0], pst[1]) if g % 2 == 0 else (psA, psB)
        mm = None
        for cl in range(32):
            pt = tp_pair[cl // 16]
            mov = AP(rawbig, g * 32 + cl, [[NT * C, P], [C, NT]])
            mm = te.matmul(pt[:, (cl % 16) * 32:(cl % 16) * 32 + 32], wtoep[hb2][:, cl * P:(cl + 1) * P], mov, start=True, stop=True)
        st_tp[g] = ST.inc(mm)
        ve.wait_ge(ST.h, st_tp[g])
        ve.tensor_tensor(yibig[:, g * 1024:g * 1024 + 512], yibig[:, g * 1024:g * 1024 + 512], tp_pair[0][:, 0:512], ALU.add)
        SV.inc(ve.tensor_tensor(yibig[:, g * 1024 + 512:(g + 1) * 1024], yibig[:, g * 1024 + 512:(g + 1) * 1024], tp_pair[1][:, 0:512], ALU.add))
        sv_tev[g] = vfence()
    sv_tev_all = SV.v
    if "yi" in dbg_o:
        sy.wait_ge(SV.h, sv_tev_all)
        qDBG.inc(sy.dma_start(out=dbg_o["yi"], in_=yibig[:]), 16)
    if "raw1" in dbg_o:
        sy.wait_ge(ST.h, st_tp[NGRP - 1])
        qDBG.inc(sy.dma_start(out=AP(dbg_o["raw1"].tensor, 0, [[C, P], [C * P, NT], [1, C]]),
                            in_=AP(rawbig, 0, [[NT * C, P], [C, NT], [1, C]])), 16)

    sy.wait_ge(SV.h, sv_r2[NT - 1])
    for k in range(4):
        sy.wait_ge(qSPX[k].h, qSPX[k].v)
    sq_xnr = {}; sg_xat = {}; ss_ln2 = {}; sq_xat = {}; sq_ln2 = {}
    sv_a3 = {}
    for g in range(NT // GW):
        tiles = range(g * GW, (g + 1) * GW)
        for i in tiles:
            sl = i % GW
            if i >= GW:
                sy.wait_ge(qSPA[sl].h, qSPA[sl].v)
                sy.wait_ge(SS.h, ss_ln2[i - GW])
            sq_xnr[i] = qLD2[sl].inc(sy.dma_start(out=xnre[sl][:], in_=xn_sp[i * P:(i + 1) * P, :]), 16)
        for i in tiles:
            sl = i % GW
            gp.wait_ge(qLD2[sl].h, sq_xnr[i])
            if i == 0:
                gp.wait_ge(SV.h, sv_tev_all)
            yi_view = AP(yibig, i, [[NT * C, P], [32, C]])
            sg_xat[i] = SG.inc(gp.tensor_tensor(xnre[sl][:], xnre[sl][:], yi_view, ALU.add))
        for i in tiles:
            sl = i % GW
            ve.wait_ge(SG.h, sg_xat[i])
            SV.inc(ve.bn_stats(stA_g[:, sl * 12:sl * 12 + 6], xnre[sl][:, 0:512]))
            svb = SV.inc(ve.bn_stats(stA_g[:, sl * 12 + 6:sl * 12 + 12], xnre[sl][:, 512:1024]))
            ve.wait_ge(SV.h, svb)
            sv_a3[i] = SV.inc(ve.bn_aggr(mv0g[:, 2 * sl:2 * sl + 2], stA_g[:, sl * 12:sl * 12 + 12]))
        sc.wait_ge(SV.h, sv_a3[tiles[-1]])
        sv_nm3 = rsig_group(mv0g, ceps5[:], rsg3g, nm3g)
        sc.wait_ge(SV.h, sv_nm3)
        for i in tiles:
            sl = i % GW
            if i >= GW:
                sc.wait_ge(qSPL[sl].h, qSPL[sl].v)
            ss_ln2[i] = SS.inc(sc.activation(ln2b[sl][:], xnre[sl][:], AF.Identity,
                                             scale=rsg3g[:, sl:sl + 1], bias=nm3g[:, sl:sl + 1]))
        for i in tiles:
            sl = i % GW
            sy.wait_ge(SS.h, ss_ln2[i])
            sq_xat[i] = qSPA[sl].inc(sy.dma_start(out=xattn_sp[i * P:(i + 1) * P, :], in_=xnre[sl][:]), 16)
            sq_ln2[i] = qSPL[sl].inc(sy.dma_start(out=ln2_sp[i * P:(i + 1) * P, :], in_=ln2b[sl][:]), 16)
    if "xattn" in dbg_o:
        for k in range(4):
            sy.wait_ge(qSPA[k].h, qSPA[k].v)
        qDBG.inc(sy.dma_start(out=dbg_o["xattn"], in_=xattn_sp), 16)
    if "ln2" in dbg_o:
        for k in range(4):
            sy.wait_ge(qSPL[k].h, qSPL[k].v)
        qDBG.inc(sy.dma_start(out=dbg_o["ln2"], in_=ln2_sp), 16)

    sc.wait_ge(ST.h, st_tp[NGRP - 1])
    sc.wait_ge(qCAST.h, gq_casts)
    aq_w1t = {}
    for kb in range(KB):
        aq_w1t[kb] = qW1T.inc(sc.dma_start(out=fc1wT[:, kb * HD:(kb + 1) * HD], in_=fc1w_bf[:, kb * P:(kb + 1) * P], transpose=True), 16)
    sc.wait_ge(SG.h, sg_xat[NT - 1])
    aq_w2t = {}
    for hb in range(HB):
        aq_w2t[hb] = qW2T.inc(sc.dma_start(out=fc2wT[:, hb * C:(hb + 1) * C], in_=fc2w_bf[:, hb * P:(hb + 1) * P], transpose=True), 16)
    sv_n2bf = SV.inc(ve.tensor_copy(n2b_bf[:], n2b_pb[:, 0:KB]))
    te.wait_ge(SV.h, sv_n2bf)
    te.wait_ge(SV.h, sv_yi_all)
    te.wait_ge(qW1T.h, aq_w1t[KB - 1])
    for hb in range(HB):
        for kb in range(KB):
            ins_mm = te.matmul(psc[0][:, hb:hb + 1], fc1wT[:, kb * HD + hb * P:kb * HD + (hb + 1) * P],
                               n2b_bf[:, kb:kb + 1], start=(kb == 0), stop=(kb == KB - 1))
    st_bf = ST.inc(ins_mm)
    ve.wait_ge(ST.h, st_bf)
    SV.inc(ve.tensor_add(b1p[:, 0:HB], psc[0][:, 0:HB], fc1b_pb[:, 0:HB]))
    sv_b1p = vfence()
    sv_w1s = {}
    for kb in range(KB):
        sv_w1s[kb] = SV.inc(ve.tensor_scalar(fc1wT[:, kb * HD:(kb + 1) * HD], fc1wT[:, kb * HD:(kb + 1) * HD], n2w_pb[:, kb:kb + 1], None, ALU.mult))
    sv_wscale = vfence()

    for k in range(4):
        sc.wait_ge(qSPL[k].h, qSPL[k].v)
        sc.wait_ge(qSPR[k].h, qSPR[k].v)
    sc.wait_ge(SG.h, sg_xat[NT - 1])
    aq_r2t = {}; ss_xn2 = {}; sq_xn2 = {}; sv_mx = {}
    for tt in range(T3):
        b = tt % 2
        for kb in range(KB):
            if tt >= 2:
                sc.wait_ge(SS.h, ss_xn2[tt - 2])
            aq_r2t[(tt, kb)] = qR2T[b].inc(sc.dma_start(
                out=r2T[b][:, kb * NTOK:(kb + 1) * NTOK],
                in_=raw2_sp[tt * NTOK:(tt + 1) * NTOK, kb * P:(kb + 1) * P], transpose=True), 16)
        sc.wait_ge(qR2T[b].h, aq_r2t[(tt, KB - 1)])
        if tt >= 1:
            sc.wait_ge(qX2S.h, sq_xn2[tt - 1])
            sc.wait_ge(SV.h, sv_mx[tt - 1])
        for kb in range(KB):
            ins_aff = sc.activation(xqT[:, kb * NTOK:(kb + 1) * NTOK], r2T[b][:, kb * NTOK:(kb + 1) * NTOK],
                                    AF.Identity, scale=g_pb[:, kb:kb + 1], bias=bta_pb[:, kb:kb + 1])
        ss_xn2[tt] = SS.inc(ins_aff)
        ve.wait_ge(SS.h, ss_xn2[tt])
        svt = SV.inc(ve.reduce_max(rmx[:], xqT[:], axis=mybir.AxisListType.X, apply_absolute_value=True))
        ve.wait_ge(SV.h, svt)
        SV.inc(ve.tensor_max(smax[:], smax[:], rmx[:]))
        sv_mx[tt] = vfence()
        sy.wait_ge(SS.h, ss_xn2[tt])
        sq_xn2[tt] = qX2S.inc(sy.dma_start(
            out=AP(xn2_sp.tensor, tt * NTOK, [[TLOC, P], [P * TLOC, KB], [1, NTOK]]),
            in_=AP(xqT, 0, [[KB * NTOK, P], [NTOK, KB], [1, NTOK]])), 16)
    sy.wait_ge(SV.h, sv_mx[T3 - 1])
    sq_gw = qSH.inc(sy.dma_start(out=cc_gin[:, :], in_=smax[:]), 16)
    gp.wait_ge(qSH.h, sq_gw)
    cc2 = CCS.inc(gp.collective_compute(
        "AllReduce", ALU.max, replica_groups=[list(range(8))],
        ins=[cc_gin], outs=[cc_gout]))
    sy.wait_ge(CCS.h, cc2)
    sq_gr = qSH.inc(sy.dma_start(out=srow[:], in_=cc_gout), 16)
    ve.wait_ge(qSH.h, sq_gr)
    svt = SV.inc(ve.reduce_max(sc1[:, 3:4], srow[:], axis=mybir.AxisListType.X))
    ve.wait_ge(SV.h, svt)
    svt = SV.inc(ve.reciprocal(sc1[:, 4:5], sc1[:, 3:4]))
    ve.wait_ge(SV.h, svt)
    svt = SV.inc(ve.tensor_scalar(sc1[:, 5:6], sc1[:, 4:5], 128.0, None, ALU.mult))
    ve.wait_ge(SV.h, svt)
    svt = SV.inc(ve.tensor_tensor(sc1[:, 6:7], sc1[:, 3:4], sc1[:, 7:8], ALU.mult))
    ve.wait_ge(SV.h, svt)
    SV.inc(ve.tensor_scalar(sc1[:, 6:7], sc1[:, 6:7], 1.0 / 128.0, None, ALU.mult))
    sv_q = vfence()
    sy.wait_ge(SV.h, sv_q)
    sq_qw = qSH.inc(sy.dma_start(out=scal_hbm[:, 4:7], in_=sc1[:, 4:7]), 16)
    sy.wait_ge(qSH.h, sq_qw)
    qSH.inc(sy.dma_start(out=rqbc[:], in_=AP(scal_hbm.tensor, 5, [[0, P], [1, 1]])), 16)
    sq_ysc = qSH.inc(sy.dma_start(out=ysc[:], in_=AP(scal_hbm.tensor, 6, [[0, P], [1, 1]])), 16)
    if "scal" in dbg_o:
        sq_sc16 = qSH.inc(sy.dma_start(out=scal_hbm[:, 16:32], in_=sc1[:, 0:16]), 16)
        sy.wait_ge(qSH.h, sq_sc16)
        qDBG.inc(sy.dma_start(out=dbg_o["scal"], in_=scal_hbm), 16)

    sy.wait_ge(qWQ.h, sq_wq)
    sq_wql = qWQL.inc(sy.dma_start(out=wq_sb[:], in_=wq_hbm), 16)
    st_ffn = {}; ss_g = {}; sq_g = {}
    sq_x2l = {}
    sy.wait_ge(SS.h, ss_xn2[T3 - 1])
    sy.wait_ge(qX2S.h, qX2S.v)
    for tt in range(T3):
        b = tt % 2
        if tt >= 2:
            sy.wait_ge(SV.h, sv_xq[tt - 2])
        sq_x2l[tt] = qX2L[b].inc(sy.dma_start(
            out=AP(r2T[b], 0, [[KB * NTOK, P], [NTOK, KB], [1, NTOK]]),
            in_=AP(xn2_sp.tensor, tt * NTOK, [[TLOC, P], [P * TLOC, KB], [1, NTOK]])), 16)
        ve.wait_ge(qX2L[b].h, sq_x2l[tt])
        ve.wait_ge(qSH.h, sq_ysc)
        if tt >= 1:
            ve.wait_ge(ST.h, st_ffn[(tt - 1, KB - 1)])
        ve.tensor_scalar(xqT[:], r2T[b][:], rqbc[:], 128.0 - FFN_EPS, ALU.mult, ALU.min)
        if tt == 0:
            sv_xq = {}
        SV.inc(ve.tensor_scalar(xqT[:], xqT[:], -(128.0 - FFN_EPS), None, ALU.max))
        sv_xq[tt] = vfence()
        te.wait_ge(SV.h, sv_xq[tt])
        te.wait_ge(qWQL.h, sq_wql)
        for ub in range(KB):
            pf = pst[ub % 2]
            prev = (tt, ub - 2) if ub >= 2 else ((tt - 1, KB - 2 + ub) if tt > 0 else None)
            if prev is not None:
                te.wait_ge(SS.h, ss_g[prev])
            for kb in range(KB):
                mm = te.matmul(pf[:, 0:NTOK], wq_sb[:, kb * C + ub * P:kb * C + (ub + 1) * P],
                               xqT[:, kb * NTOK:(kb + 1) * NTOK], start=(kb == 0), stop=(kb == KB - 1))
            st_ffn[(tt, ub)] = ST.inc(mm)
            sc.wait_ge(ST.h, st_ffn[(tt, ub)])
            sc.wait_ge(qSH.h, sq_ysc)
            if prev is not None:
                sc.wait_ge(qG[prev[1] % 2].h, sq_g[prev])
            ss_g[(tt, ub)] = SS.inc(sc.activation(gst[ub % 2][:], pf[:, 0:NTOK], AF.Silu, scale=ysc[:]))
            sy.wait_ge(SS.h, ss_g[(tt, ub)])
            sq_g[(tt, ub)] = qG[ub % 2].inc(sy.dma_start(
                out=AP(g_sp.tensor, ub * P * TLOC + tt * NTOK, [[TLOC, P], [1, NTOK]]),
                in_=gst[ub % 2][:]), 16)
    sq_g_all = (qG[0].v, qG[1].v)
    if "xn2" in dbg_o:
        qDBG.inc(sy.dma_start(out=dbg_o["xn2"], in_=xn2_sp), 16)
    if "gsp" in dbg_o:
        qDBG.inc(sy.dma_start(out=dbg_o["gsp"], in_=g_sp), 16)

    last_ffn = st_ffn[(T3 - 1, KB - 1)]
    sc.wait_ge(ST.h, last_ffn)
    sc.wait_ge(qG[0].h, sq_g_all[0])
    sc.wait_ge(qG[1].h, sq_g_all[1])
    sy.wait_ge(qG[0].h, sq_g_all[0])
    sy.wait_ge(qG[1].h, sq_g_all[1])
    for k in range(4):
        sy.wait_ge(qSPA[k].h, qSPA[k].v)
    aq_l2 = {}; st_f1 = {}; ss_ge = {}; st_f2 = {}
    sq_sg = {}; sq_gl = {}; sq_xt = {}; st_tx = {}; sv_xe = {}; sv_pg = {}; sv_m = {}; sq_ot = {}
    te.wait_ge(SS.h, ss_g[(T3 - 1, KB - 1)])
    te.wait_ge(SV.h, sv_tev_all)
    te.wait_ge(SV.h, sv_yi_all)
    te.wait_ge(SV.h, sv_b1p)
    def ln2t_load(tt):
        for kb in range(KB):
            if tt > 0:
                sc.wait_ge(ST.h, st_f1[(tt - 1, HB - 1)])
            aq_l2[(tt, kb)] = qL2T.inc(sc.dma_start(
                out=ln2T[:, kb * NTOK:(kb + 1) * NTOK],
                in_=ln2_sp[tt * NTOK:(tt + 1) * NTOK, kb * P:(kb + 1) * P], transpose=True), 16)
    ln2t_load(0)
    for tt in range(T3):
        te.wait_ge(qL2T.h, aq_l2[(tt, KB - 1)])
        if tt == 0:
            te.wait_ge(SV.h, sv_wscale)
            te.wait_ge(qW2T.h, aq_w2t[HB - 1])
        for mb in range(HB):
            pf = pst[mb % 2]
            prev = (tt, mb - 2) if mb >= 2 else ((tt - 1, HB - 2 + mb) if tt > 0 else None)
            if prev is not None:
                te.wait_ge(SS.h, ss_ge[prev])
            for kb in range(KB):
                mm = te.matmul(pf[:, 0:NTOK], fc1wT[:, kb * HD + mb * P:kb * HD + (mb + 1) * P],
                               ln2T[:, kb * NTOK:(kb + 1) * NTOK], start=(kb == 0), stop=(kb == KB - 1))
            st_f1[(tt, mb)] = ST.inc(mm)
            sc.wait_ge(ST.h, st_f1[(tt, mb)])
            if tt > 0:
                sc.wait_ge(ST.h, st_f2[(tt - 1, KB - 1)])
            ss_ge[(tt, mb)] = SS.inc(sc.activation(gbig[:, mb * NTOK:(mb + 1) * NTOK], pf[:, 0:NTOK],
                                                   AF.Gelu, bias=b1p[:, mb:mb + 1]))
        if tt + 1 < T3:
            ln2t_load(tt + 1)
        te.wait_ge(SS.h, ss_ge[(tt, HB - 1)])
        for cb in range(KB):
            s2 = cb % 2
            pg = psA if s2 == 0 else psB
            prev = (tt, cb - 2) if cb >= 2 else ((tt - 1, KB - 2 + cb) if tt > 0 else None)
            if prev is not None:
                sy.wait_ge(ST.h, st_tx[prev])
                sy.wait_ge(SV.h, sv_m[prev])
            sq_sg[(tt, cb)] = qXA.inc(sy.dma_start(
                out=stg[s2][:], in_=AP(xattn_sp.tensor, (tt * NTOK) * C + cb * P, [[C, P], [P * C, 4], [1, P]])), 16)
            sq_gl[(tt, cb)] = qM[s2].inc(sy.dma_start(
                out=gld[s2][:], in_=AP(g_sp.tensor, cb * P * TLOC + tt * NTOK, [[TLOC, P], [1, NTOK]])), 16)
            sq_xt[(tt, cb)] = qXR.inc(sy.dma_start(
                out=xTld[s2][:], in_=AP(xT_in.tensor, cb * P * TLOC + tt * NTOK, [[TLOC, P], [1, NTOK]])), 16)
            te.wait_ge(qXA.h, sq_sg[(tt, cb)])
            if prev is not None:
                te.wait_ge(SV.h, sv_xe[prev])
            for j in range(4):
                mmt = te.transpose(psc[1][:, s2 * NTOK + j * P:s2 * NTOK + (j + 1) * P],
                                   stg[s2][:, j * P:(j + 1) * P], identf[:])
            st_tx[(tt, cb)] = ST.inc(mmt)
            if prev is not None:
                te.wait_ge(SV.h, sv_pg[prev])
            for hb in range(HB):
                mm = te.matmul(pg[:, 0:NTOK], fc2wT[:, hb * C + cb * P:hb * C + (cb + 1) * P],
                               gbig[:, hb * NTOK:(hb + 1) * NTOK], start=(hb == 0), stop=(hb == HB - 1))
            st_f2[(tt, cb)] = ST.inc(mm)
            ve.wait_ge(ST.h, st_tx[(tt, cb)])
            sv_xe[(tt, cb)] = SV.inc(ve.tensor_copy(xaT[s2][:], psc[1][:, s2 * NTOK:(s2 + 1) * NTOK]))
            ve.wait_ge(ST.h, st_f2[(tt, cb)])
            ve.wait_ge(qM[s2].h, sq_gl[(tt, cb)])
            ve.wait_ge(qXR.h, sq_xt[(tt, cb)])
            if prev is not None:
                ve.wait_ge(qOUT.h, sq_ot[prev])
            sv_pg[(tt, cb)] = SV.inc(ve.scalar_tensor_tensor(
                ostT[s2][:], pg[:, 0:NTOK], fc2b_pb[:, cb:cb + 1], xaT[s2][:], ALU.add, ALU.add))
            ve.tensor_mul(ostT[s2][:], ostT[s2][:], gld[s2][:])
            SV.inc(ve.tensor_add(ostT[s2][:], ostT[s2][:], xTld[s2][:]))
            sv_m[(tt, cb)] = vfence()
            sy.wait_ge(SV.h, sv_m[(tt, cb)])
            sq_ot[(tt, cb)] = qOUT.inc(sy.dma_start(
                out=AP(out_hbm.tensor, cb * P * TLOC + tt * NTOK, [[TLOC, P], [1, NTOK]]),
                in_=ostT[s2][:]), 16)

    for q in ALLQ:
        if q.v:
            sy.wait_ge(q.h, q.v)

    ctx.close()
    return nc


def make_in_maps(inputs):
    x = np.ascontiguousarray(np.asarray(inputs["L0_attn_x"], dtype=np.float32))
    def f32(k):
        return np.ascontiguousarray(np.asarray(inputs[k], dtype=np.float32))
    base = {
        "log_tau": f32("L0_attn_log_tau").reshape(NST, 1),
        "log_dt": f32("L0_attn_log_dt").reshape(1, 1),
        "mix": f32("L0_attn_mix"),
        "dvec": f32("L0_attn_D").reshape(1, C),
        "n1w": f32("L0_attn_norm1_w").reshape(1, C),
        "n1b": f32("L0_attn_norm1_b").reshape(1, C),
        "n2w": f32("L0_attn_norm2_w").reshape(1, C),
        "n2b": f32("L0_attn_norm2_b").reshape(1, C),
        "fc1w": f32("L0_attn_fc1_w"),
        "fc1b": f32("L0_attn_fc1_b").reshape(1, HD),
        "fc2w": f32("L0_attn_fc2_w"),
        "fc2b": f32("L0_attn_fc2_b").reshape(1, C),
        "ffnw": f32("L0_ffn_weight"),
        "lng": f32("L0_ffn_ln_gamma").reshape(1, C),
        "lnb": f32("L0_ffn_ln_beta").reshape(1, C),
        "iota128": np.arange(P, dtype=np.float32).reshape(1, P),
        "iota128p1": (np.arange(P, dtype=np.float32) + 1).reshape(1, P),
        "rev127": (127.0 - np.arange(P, dtype=np.float32)).reshape(P, 1),
        "j128": (128.0 * np.arange(NT + 1, dtype=np.float32)).reshape(1, NT + 1),
    }
    in_maps = []
    for core in range(8):
        b, hf = core // 2, core % 2
        m = dict(base)
        m["x_sh"] = np.ascontiguousarray(x[b, hf * TLOC:(hf + 1) * TLOC, :])
        m["xT_sh"] = np.ascontiguousarray(m["x_sh"].T)
        m["emask"] = np.array([[1.0 if hf == 0 else 0.0]], dtype=np.float32)
        m["omask"] = np.array([[1.0 if hf == 1 else 0.0]], dtype=np.float32)
        in_maps.append(m)
    return in_maps


def assemble(results):
    out = np.empty((4, 2 * TLOC, C), dtype=np.float32)
    for core in range(8):
        b, hf = core // 2, core % 2
        out[b, hf * TLOC:(hf + 1) * TLOC, :] = np.asarray(results[core]["out_sh"]).T
    return out


if __name__ == "__main__":
    nc = build()
    print("built, instructions:", nc.n_instructions())


_NC_CACHE = {}


def kernel(**inputs):
    from concourse.bass_utils import run_bass_kernel_spmd
    if "nc" not in _NC_CACHE:
        _NC_CACHE["nc"] = build()
    nc = _NC_CACHE["nc"]
    in_maps = make_in_maps(inputs)
    res = run_bass_kernel_spmd(nc, in_maps, core_ids=list(range(8)))
    return assemble(res.results)


# version 06 (final)
# speedup vs baseline: 1.0597x; 1.0597x over previous
import numpy as np
import concourse.bass as bass
import concourse.mybir as mybir
from concourse.bass_types import AP
from contextlib import ExitStack

F32 = mybir.dt.float32
BF16 = mybir.dt.bfloat16
AF = mybir.ActivationFunctionType
ALU = mybir.AluOpType

P = 128
C = 1024
HD = 4096
TLOC = 4096
NT = 32
KB = 8
HB = 32
NTOK = 512
T3 = 8
NST = 64
LN0_EPS = 1e-5
LN1_EPS = 1e-5
LNF_EPS = 1e-3
LN2_EPS = 1e-5
FFN_EPS = 1e-5
MAGIC = 12582912.0
SB_TOP = 229344


def build(debug=()):
    nc = bass.Bass("TRN2", target_bir_lowering=False, detect_race_conditions=False)
    dbg = set(debug)

    def din(name, shape, dt=F32):
        return nc.dram_tensor(name, shape, dt, kind="ExternalInput").ap()

    def dint(name, shape, dt=F32):
        return nc.dram_tensor(name, shape, dt, kind="Internal").ap()

    x_in = din("x_sh", [TLOC, C])
    log_tau = din("log_tau", [NST, 1])
    log_dt = din("log_dt", [1, 1])
    mix_in = din("mix", [C, NST])
    d_in = din("dvec", [1, C])
    n1w_in = din("n1w", [1, C]); n1b_in = din("n1b", [1, C])
    n2w_in = din("n2w", [1, C]); n2b_in = din("n2b", [1, C])
    fc1w_in = din("fc1w", [HD, C]); fc1b_in = din("fc1b", [1, HD])
    fc2w_in = din("fc2w", [C, HD]); fc2b_in = din("fc2b", [1, C])
    ffnw_in = din("ffnw", [C, C])
    lng_in = din("lng", [1, C]); lnb_in = din("lnb", [1, C])
    emask_in = din("emask", [1, 1])
    omask_in = din("omask", [1, 1])
    iota128 = din("iota128", [1, P])
    iota128p1 = din("iota128p1", [1, P])
    rev127_in = din("rev127", [P, 1])
    j128_in = din("j128", [1, NT + 1])

    out_hbm = nc.dram_tensor("out_sh", [TLOC, C], F32, kind="ExternalOutput").ap()
    dbg_o = {}
    for name, shape, dt in [
        ("xn", [TLOC, C], F32), ("raw1", [TLOC, C], BF16), ("raw2", [TLOC, C], BF16),
        ("yi", [P, NT * C], BF16), ("xattn", [TLOC, C], F32), ("ln2", [TLOC, C], BF16),
        ("gsp", [C, TLOC], BF16), ("mlp", [C, TLOC], BF16), ("xn2", [KB, P * TLOC], BF16),
        ("wq", [P, KB * C], BF16), ("ksrc", [C, 256], BF16), ("bpre", [P, C], BF16),
        ("sm", [P, NT * 512], BF16), ("scal", [1, 64], F32),
    ]:
        if name in dbg:
            dbg_o[name] = nc.dram_tensor("dbg_" + name, shape, dt, kind="ExternalOutput").ap()

    xn_sp = dint("xn_sp", [TLOC, C])
    raw2_sp = dint("raw2_sp", [TLOC, C], BF16)
    xattn_sp = dint("xattn_sp", [TLOC, C])
    ln2_sp = dint("ln2_sp", [TLOC, C], BF16)
    ksrc_hbm = dint("ksrc_hbm", [C, 256], BF16)
    mix_bf = dint("mix_bf", [C, P], BF16)
    mixsm_hbm = dint("mixsm_hbm", [512, P], BF16)
    fc1w_bf = dint("fc1w_bf", [HD, C], BF16)
    fc2w_bf = dint("fc2w_bf", [C, HD], BF16)
    wq_hbm = dint("wq_hbm", [P, KB * C], BF16)
    xn2_sp = dint("xn2_sp", [KB, P * TLOC], BF16)
    g_sp = dint("g_sp", [C, TLOC], BF16)
    mlp_sp = dint("mlp_sp", [C, TLOC], BF16)
    bpre_cm = dint("bpre_cm", [C, P], BF16)
    scal_hbm = dint("scal_hbm", [1, 64])
    pbc_hbm = dint("pbc_hbm", [1, NST])
    cc_gin = dint("cc_gin", [1, P]); cc_gout = dint("cc_gout", [1, P])
    cc_sin = dint("cc_sin", [P, 512]); cc_sout = dint("cc_sout", [P, 512])

    SB_BASE = 18400

    def sb_at(name, shape, dt, offset):
        return nc.alloc_sbuf_tensor_at(name, shape, dt, offset=offset)

    def mk_region(lo, size):
        return [lo, lo + size]

    def alloc(region, name, shape, dt=F32):
        nbytes = int(np.prod(shape[1:])) * mybir.dt.size(dt)
        o = region[0]
        region[0] = (o + nbytes + 31) & ~31
        assert region[0] <= region[1], f"{name}: overflow {region}"
        return sb_at(name, shape, dt, o)

    RP = mk_region(SB_BASE, 20480)
    RS0 = RP[1]
    RS_SZ = 26624
    RC0 = RS0 + RS_SZ
    RC_SZ = 32768
    RA0 = RC0 + RC_SZ
    RB0 = RA0 + 65536
    assert RB0 + 65536 <= SB_TOP, (RB0 + 65536, SB_TOP)

    einT = alloc(RP, "einT", [P, NST], BF16)
    aoutD = alloc(RP, "aoutD", [P, P], BF16)
    mixT_sm = alloc(RP, "mixT_sm", [P, 512], BF16)
    mixpT_sm = alloc(RP, "mixpT_sm", [P, 512], BF16)
    bm1 = alloc(RP, "bm1", [P, 512], BF16)
    zqb = alloc(RP, "zqb", [P, 512], BF16)
    b1g = alloc(RP, "b1g", [P, 512], BF16)
    s_raw = alloc(RP, "s_raw", [P, 512], F32)
    s_tmp = alloc(RP, "s_tmp", [P, 512], F32)
    geoj = alloc(RP, "geoj", [P, NT + 1], F32)
    npj = alloc(RP, "npj", [P, NT + 1], F32)
    bpre = alloc(RP, "bpre", [P, C], BF16)
    w1_pb = alloc(RP, "w1_pb", [P, 32], F32)
    b1_pb = alloc(RP, "b1_pb", [P, 32], F32)
    dw1_pb = alloc(RP, "dw1_pb", [P, KB], F32)
    d_pb = alloc(RP, "d_pb", [P, 32], F32)
    n2w_pb = alloc(RP, "n2w_pb", [P, 32], F32)
    n2b_pb = alloc(RP, "n2b_pb", [P, 32], F32)
    g_pb = alloc(RP, "g_pb", [P, 32], F32)
    bta_pb = alloc(RP, "bta_pb", [P, 32], F32)
    fc1b_pb = alloc(RP, "fc1b_pb", [P, HB], F32)
    fc2b_pb = alloc(RP, "fc2b_pb", [P, 32], F32)
    b1p = alloc(RP, "b1p", [P, HB], F32)
    w1sm = alloc(RP, "w1sm", [P, 512], BF16)
    b1sm = alloc(RP, "b1sm", [P, 512], BF16)
    ltau = alloc(RP, "ltau", [P, 1], F32)
    negp = alloc(RP, "negp", [P, 1], F32)
    tauv = alloc(RP, "tauv", [P, 1], F32)
    dec = alloc(RP, "dec", [P, 1], F32)
    geoL = alloc(RP, "geoL", [P, 1], F32)
    dtbc = alloc(RP, "dtbc", [P, 1], F32)
    emask = alloc(RP, "emask", [P, 1], F32)
    omask = alloc(RP, "omask", [P, 1], F32)
    rqbc = alloc(RP, "rqbc", [P, 1], F32)
    ysc = alloc(RP, "ysc", [P, 1], F32)
    rev127 = alloc(RP, "rev127", [P, 1], F32)
    io2_bc = alloc(RP, "io2_bc", [P, P], F32)
    j128_bc = alloc(RP, "j128_bc", [P, NT + 1], F32)
    prow_bc = alloc(RP, "prow_bc", [P, NST], F32)
    smax = alloc(RP, "smax", [P, 1], F32)
    srow = alloc(RP, "srow", [1, P], F32)
    sc1 = alloc(RP, "sc1", [1, 16], F32)
    ajb = [alloc(RP, f"ajb{k}", [P, P], BF16) for k in range(2)]
    agjb = [alloc(RP, f"agjb{k}", [P, P], BF16) for k in range(2)]
    mv0 = alloc(RP, "mv0", [P, 2], F32)
    mv1 = alloc(RP, "mv1", [P, 2], F32)
    stA = alloc(RP, "stA", [P, 12], F32)
    r1 = alloc(RP, "r1", [P, 1], F32); r2 = alloc(RP, "r2", [P, 1], F32)
    r3 = alloc(RP, "r3", [P, 1], F32); r4 = alloc(RP, "r4", [P, 1], F32)
    r5 = alloc(RP, "r5", [P, 1], F32)
    rsg0 = alloc(RP, "rsg0", [P, 1], F32); nm0 = alloc(RP, "nm0", [P, 1], F32)
    rsg1 = alloc(RP, "rsg1", [P, 1], F32); nm1 = alloc(RP, "nm1", [P, 1], F32)
    rsg2 = alloc(RP, "rsg2", [P, 1], F32); nm2 = alloc(RP, "nm2", [P, 1], F32)
    rsg3 = alloc(RP, "rsg3", [P, 1], F32); nm3 = alloc(RP, "nm3", [P, 1], F32)
    rmx = alloc(RP, "rmx", [P, 1], F32)
    stA_g = alloc(RP, "stA_g", [P, 48], F32)
    mv0g = alloc(RP, "mv0g", [P, 8], F32)
    mv1g = alloc(RP, "mv1g", [P, 8], F32)
    rsg0g = alloc(RP, "rsg0g", [P, 4], F32)
    nm0g = alloc(RP, "nm0g", [P, 4], F32)
    rsg1g = alloc(RP, "rsg1g", [P, 4], F32)
    nm1g = alloc(RP, "nm1g", [P, 4], F32)
    rsg2g = alloc(RP, "rsg2g", [P, 4], F32)
    nm2g = alloc(RP, "nm2g", [P, 4], F32)
    rsg3g = alloc(RP, "rsg3g", [P, 4], F32)
    nm3g = alloc(RP, "nm3g", [P, 4], F32)
    rt4 = alloc(RP, "rt4", [P, 4], F32)
    vd = alloc(RP, "vd", [P, 1], F32)
    ident = alloc(RP, "ident", [P, P], BF16)
    pcol1 = alloc(RP, "pcol1", [P, 1], F32)
    ceps5 = alloc(RP, "ceps5", [P, 1], F32)
    n2b_bf = alloc(RP, "n2b_bf", [P, KB], BF16)
    ceps3 = alloc(RP, "ceps3", [P, 1], F32)

    S0 = mk_region(RS0, RS_SZ)
    ks_st = alloc(S0, "ks_st", [P, 256], BF16)
    mixT = alloc(S0, "mixT", [P, C], BF16)
    io1_bc = alloc(S0, "io1_bc", [P, P], F32)
    ek32 = alloc(S0, "ek32", [P, P], F32)
    ekbf = alloc(S0, "ekbf", [P, P], BF16)
    cume = alloc(S0, "cume", [P, P], F32)
    cumebf = alloc(S0, "cumebf", [P, P], BF16)
    onesq = alloc(S0, "onesq", [P, P], F32)
    stgpb = alloc(S0, "stgpb", [32, P], F32)
    bpst = alloc(S0, "bpst", [P, KB * P], BF16)
    junk = alloc(S0, "junk", [P, C], F32)
    wqst = alloc(S0, "wqst", [P, C], BF16)
    abs8 = alloc(S0, "abs8", [P, KB], F32)

    S1 = mk_region(RS0, RS_SZ)
    xbuf = [alloc(S1, f"xbuf{k}", [P, C], F32) for k in range(4)]
    r2buf = [alloc(S1, f"r2buf{k}", [P, C], BF16) for k in range(4)]

    S2 = mk_region(RS0, RS_SZ)
    xnre = [alloc(S2, f"xnre{k}", [P, C], F32) for k in range(4)]
    ln2b = [alloc(S2, f"ln2b{k}", [P, C], BF16) for k in range(4)]

    S3a = mk_region(RS0, RS_SZ)
    xqT = alloc(S3a, "xqT", [P, KB * NTOK], BF16)
    gst = [alloc(S3a, f"gst{k}", [P, NTOK], BF16) for k in range(2)]

    S3b = mk_region(RS0, RS_SZ)
    ln2T = alloc(S3b, "ln2T", [P, KB * NTOK], BF16)
    xatr = alloc(S3b, "xatr", [P, C], F32)
    xre = alloc(S3b, "xre", [P, C], F32)
    mlpT = alloc(S3b, "mlpT", [P, C], BF16)
    gT = alloc(S3b, "gT", [P, C], BF16)
    mst = [alloc(S3b, f"mst{k}", [P, NTOK], BF16) for k in range(2)]
    outb = alloc(S3b, "outb", [P, C], F32)

    C1 = mk_region(RC0, RC_SZ)
    smbig = alloc(C1, "smbig", [P, NT * 512], BF16)
    C2 = mk_region(RC0, RC_SZ)
    hank = [alloc(C2, f"hank{k}", [P, 32 * P], BF16) for k in range(2)]
    wtoep = [alloc(C2, f"wtoep{k}", [P, 32 * P], BF16) for k in range(2)]
    C3 = mk_region(RC0, RC_SZ)
    gbig = alloc(C3, "gbig", [P, HB * NTOK], BF16)
    C4 = mk_region(RC0, RC_SZ)
    wq_sb = alloc(C4, "wq_sb", [P, KB * C], BF16)
    r2T = [alloc(C4, f"r2T{k}", [P, KB * NTOK], BF16) for k in range(2)]

    A1 = mk_region(RA0, 65536)
    rawbig = alloc(A1, "rawbig", [P, NT * C], BF16)
    A2 = mk_region(RA0, 65536)
    fc1wT = alloc(A2, "fc1wT", [P, KB * HD], BF16)
    A3 = mk_region(RA0, 65536)
    ffst = alloc(A3, "ffst", [P, KB * C], F32)

    B1 = mk_region(RB0, 65536)
    yibig = alloc(B1, "yibig", [P, NT * C], BF16)
    B2 = mk_region(RB0, 65536)
    fc2wT = alloc(B2, "fc2wT", [P, HB * C], BF16)

    ctx = ExitStack()
    psA = ctx.enter_context(nc.psum_tensor("psA", [P, 512], F32))
    psB = ctx.enter_context(nc.psum_tensor("psB", [P, 512], F32))
    psc = [ctx.enter_context(nc.psum_tensor(f"psc{k}", [P, 1024], F32)) for k in range(2)]
    pst = [ctx.enter_context(nc.psum_tensor(f"pst{k}", [P, 512], F32)) for k in range(2)]

    class Sem:
        def __init__(self, name):
            self.h = nc.alloc_semaphore(name)
            self.v = 0
        def inc(self, instr, k=1):
            instr.then_inc(self.h, k)
            self.v += k
            return self.v
    SV = Sem("sv"); SS = Sem("ss"); SG = Sem("sg"); ST = Sem("stm"); CCS = Sem("ccs")
    qCAST = Sem("qcast"); qWSM = Sem("qwsm"); qP0 = Sem("qp0"); qPB = Sem("qpb")
    qFST = Sem("qfst"); qSH = Sem("qsh"); qKS = Sem("qks"); qBP = Sem("qbp")
    qWQ = Sem("qwq"); qMXT = Sem("qmxt"); qBPT = Sem("qbpt"); qWQL = Sem("qwql")
    qLD = [Sem(f"qld{k}") for k in range(4)]
    qSPX = [Sem(f"qspx{k}") for k in range(4)]
    qSPR = [Sem(f"qspr{k}") for k in range(4)]
    qLD2 = [Sem(f"qld2{k}") for k in range(4)]
    qSPA = [Sem(f"qspa{k}") for k in range(4)]
    qSPL = [Sem(f"qspl{k}") for k in range(4)]
    qHK = [Sem(f"qhk{k}") for k in range(2)]
    qW1T = Sem("qw1t"); qW2T = Sem("qw2t")
    qR2T = [Sem(f"qr2t{k}") for k in range(2)]
    qX2S = Sem("qx2s")
    qX2L = [Sem(f"qx2l{k}") for k in range(2)]
    qG = [Sem(f"qg{k}") for k in range(2)]
    qM = [Sem(f"qm{k}") for k in range(2)]
    qL2T = Sem("ql2t"); qMT = Sem("qmt"); qGT = Sem("qgt")
    qXA = Sem("qxa"); qXR = Sem("qxr"); qOUT = Sem("qout"); qDBG = Sem("qdbg")
    ALLQ = [qCAST, qWSM, qP0, qPB, qFST, qSH, qKS, qBP, qWQ, qMXT, qBPT, qWQL,
            qW1T, qW2T, qX2S, qL2T, qMT, qGT, qXA, qXR, qOUT, qDBG] + \
        qLD + qSPX + qSPR + qLD2 + qSPA + qSPL + qHK + qR2T + qX2L + qG + qM

    block = ctx.enter_context(nc.Block())
    sy, sc, ve, te, gp = nc.sync, nc.scalar, nc.vector, nc.tensor, nc.gpsimd

    def vfence():
        return SV.inc(ve.tensor_copy(vd[:], ceps5[:]))
    ve.memset(ceps5[:], LN0_EPS)
    ve.memset(s_raw[:], 0.0)
    ve.memset(smax[:], 0.0)
    ve.memset(geoj[:, 0:1], 0.0)
    sv_ceps = nc.alloc_semaphore("ceps")
    ve.memset(ceps3[:], LNF_EPS).then_inc(sv_ceps, 1)
    sc.wait_ge(sv_ceps, 1)

    for i in range(4):
        qCAST.inc(gp.dma_start(out=fc1w_bf[i * 1024:(i + 1) * 1024, :], in_=fc1w_in[i * 1024:(i + 1) * 1024, :]), 16)
    for i in range(4):
        qCAST.inc(gp.dma_start(out=fc2w_bf[i * 256:(i + 1) * 256, :], in_=fc2w_in[i * 256:(i + 1) * 256, :]), 16)
    qCAST.inc(gp.dma_start(out=mix_bf[:, 0:NST], in_=mix_in), 16)
    qCAST.inc(gp.dma_start(out=AP(mixsm_hbm.tensor, 0, [[P, 512], [1, NST]]), in_=mix_in[0:512, :]), 16)
    qCAST.inc(gp.dma_start(out=AP(mixsm_hbm.tensor, NST, [[P, 512], [1, NST]]), in_=mix_in[512:C, :]), 16)
    gq_casts = qCAST.v

    qP0.inc(sy.dma_start(out=ltau[0:NST, :], in_=log_tau), 16)
    qP0.inc(sy.dma_start(out=ltau[NST:P, :], in_=log_tau), 16)
    qP0.inc(sy.dma_start(out=r1[0:1, :], in_=log_dt), 16)
    qP0.inc(sy.dma_start(out=emask[:], in_=AP(emask_in.tensor, 0, [[0, P], [1, 1]])), 16)
    qP0.inc(sy.dma_start(out=omask[:], in_=AP(omask_in.tensor, 0, [[0, P], [1, 1]])), 16)
    qP0.inc(sy.dma_start(out=io1_bc[:], in_=AP(iota128.tensor, 0, [[0, P], [1, P]])), 16)
    qP0.inc(sy.dma_start(out=io2_bc[:], in_=AP(iota128p1.tensor, 0, [[0, P], [1, P]])), 16)
    qP0.inc(sy.dma_start(out=j128_bc[:], in_=AP(j128_in.tensor, 0, [[0, P], [1, NT + 1]])), 16)
    qP0.inc(sy.dma_start(out=rev127[:], in_=rev127_in), 16)
    qP0.inc(sy.dma_start(out=pcol1[:], in_=AP(iota128p1.tensor, 0, [[1, P], [1, 1]])), 16)
    pb_jobs = [(w1_pb, n1w_in, KB), (b1_pb, n1b_in, KB), (n2w_pb, n2w_in, KB),
               (n2b_pb, n2b_in, KB), (d_pb, d_in, KB), (g_pb, lng_in, KB),
               (bta_pb, lnb_in, KB), (fc2b_pb, fc2b_in, KB), (fc1b_pb, fc1b_in, HB)]
    sv_pb = SV.v
    for dst, src, nb in pb_jobs:

        sq_st = qPB.inc(sy.dma_start(out=stgpb[0:nb, :], in_=AP(src.tensor, 0, [[P, nb], [1, P]])), 16)
        ve.wait_ge(qPB.h, sq_st)
        for jb in range(4):
            sv_pb = SV.inc(ve.transpose(dst[32 * jb:32 * (jb + 1), 0:32], stgpb[0:32, 32 * jb:32 * (jb + 1)]))
        sy.wait_ge(SV.h, sv_pb)
    qWSM.inc(gp.dma_start(out=w1sm[0:NST, :], in_=AP(n1w_in.tensor, 0, [[0, NST], [1, 512]])), 16)
    qWSM.inc(gp.dma_start(out=w1sm[NST:P, :], in_=AP(n1w_in.tensor, 512, [[0, NST], [1, 512]])), 16)
    qWSM.inc(gp.dma_start(out=b1sm[0:NST, :], in_=AP(n1b_in.tensor, 0, [[0, NST], [1, 512]])), 16)
    gq_sm = qWSM.inc(gp.dma_start(out=b1sm[NST:P, :], in_=AP(n1b_in.tensor, 512, [[0, NST], [1, 512]])), 16)
    for i in range(KB):
        qFST.inc(sy.dma_start(out=ffst[:, i * C:(i + 1) * C], in_=ffnw_in[i * P:(i + 1) * P, :]), 16)
    sq_p0 = qP0.v; sq_fst = qFST.v

    sc.wait_ge(qP0.h, sq_p0)
    ve.wait_ge(qP0.h, sq_p0)
    SV.inc(ve.tensor_scalar(ident[:], io2_bc[:], pcol1[:, 0:1], None, ALU.is_equal))
    sc.activation(r2[:], ltau[:], AF.Abs)
    sc.activation(r3[:], r2[:], AF.Exp, scale=-1.0)
    sc.activation(r4[:], r3[:], AF.Ln, bias=1.0)
    ss_sp1 = SS.inc(sc.activation(r5[:], ltau[:], AF.Relu))
    ve.wait_ge(SS.h, ss_sp1)
    sv_tau = SV.inc(ve.tensor_add(tauv[:], r4[:], r5[:]))
    sc.wait_ge(SV.h, sv_tau)
    sc.activation(r2[0:1, :], r1[0:1, :], AF.Abs)
    sc.activation(r3[0:1, :], r2[0:1, :], AF.Exp, scale=-1.0)
    sc.activation(r4[0:1, :], r3[0:1, :], AF.Ln, bias=1.0)
    ss_sp2 = SS.inc(sc.activation(r5[0:1, :], r1[0:1, :], AF.Relu))
    ve.wait_ge(SS.h, ss_sp2)
    SV.inc(ve.tensor_add(r1[0:1, :], r4[0:1, :], r5[0:1, :]))
    sv_dt = vfence()
    sy.wait_ge(SV.h, sv_dt)
    sq_dtw = qSH.inc(sy.dma_start(out=scal_hbm[:, 0:1], in_=r1[0:1, :]), 16)
    sy.wait_ge(qSH.h, sq_dtw)
    sq_dtr = qSH.inc(sy.dma_start(out=dtbc[:], in_=AP(scal_hbm.tensor, 0, [[0, P], [1, 1]])), 16)
    ve.wait_ge(qSH.h, sq_dtr)
    ve.wait_ge(SV.h, sv_tau)
    SV.inc(ve.tensor_scalar(negp[:], tauv[:], dtbc[:], -1.0, ALU.mult, ALU.mult))
    sv_negp = vfence()
    sy.wait_ge(SV.h, sv_negp)
    sq_pw = qSH.inc(sy.dma_start(out=pbc_hbm[:, :], in_=negp[0:NST, :]), 16)
    sy.wait_ge(qSH.h, sq_pw)
    sq_pr = qSH.inc(sy.dma_start(out=prow_bc[:], in_=AP(pbc_hbm.tensor, 0, [[0, P], [1, NST]])), 16)

    sc.wait_ge(qSH.h, sq_pr)
    sc.activation(einT[:], prow_bc[:], AF.Exp, scale=rev127[:])
    sc.activation(aoutD[:], io2_bc[:], AF.Exp, scale=negp[:])
    sc.activation(ek32[:], io1_bc[:], AF.Exp, scale=negp[:])
    sc.activation(ekbf[:], io1_bc[:], AF.Exp, scale=negp[:])
    sc.activation(dec[:], negp[:], AF.Exp, scale=128.0)
    ss_tabs = SS.inc(sc.activation(npj[:], j128_bc[:], AF.Identity, scale=negp[:]))

    ve.wait_ge(SS.h, ss_tabs)
    ve.memset(onesq[:], 1.0)
    sv_scan = SV.inc(ve.tensor_tensor_scan(cume[:], onesq[:], ek32[:], 0.0, ALU.mult, ALU.add))
    ve.wait_ge(SV.h, sv_scan)
    sv_cume = SV.inc(ve.tensor_copy(cumebf[:], cume[:]))
    SV.inc(ve.reduce_sum(geoL[:], ek32[:], axis=mybir.AxisListType.X))
    sv_geo0 = SV.v
    sc.wait_ge(SV.h, sv_geo0)
    for j in range(NT):
        ssj = SS.inc(sc.activation(r2[:], npj[:, j:j + 1], AF.Exp))
        ve.wait_ge(SS.h, ssj)
        svj = SV.inc(ve.scalar_tensor_tensor(geoj[:, j + 1:j + 2], r2[:], geoL[:], geoj[:, j:j + 1], ALU.mult, ALU.add))
        if j < NT - 1:
            sc.wait_ge(SV.h, svj)
    sv_geoj = SV.v

    sc.wait_ge(qCAST.h, gq_casts)
    qMXT.inc(sc.dma_start(out=mixT[:], in_=mix_bf, transpose=True), 16)
    aq_mixT = qMXT.inc(sc.dma_start(out=mixT_sm[:], in_=mixsm_hbm, transpose=True), 16)

    ve.wait_ge(qMXT.h, aq_mixT)
    ve.wait_ge(qWSM.h, gq_sm)
    ve.tensor_mul(mixpT_sm[:], mixT_sm[:], w1sm[:])
    ve.tensor_mul(bm1[:], mixT_sm[:], b1sm[:])
    ve.tensor_scalar(b1g[:], b1sm[:], geoj[:, NT:NT + 1], None, ALU.mult)
    sv_mixp = SV.inc(ve.tensor_mul(dw1_pb[:], d_pb[:, 0:KB], w1_pb[:, 0:KB]))

    te.wait_ge(qMXT.h, aq_mixT)
    te.wait_ge(SV.h, sv_cume)
    sv_evac = SV.v
    ve.wait_ge(SV.h, sv_mixp)
    ve.memset(ks_st[:], 0.0)
    sq_ks = []
    for cb in range(KB):
        if cb > 0:
            te.wait_ge(SV.h, sv_evac)
        te.matmul(pst[0][:, 0:P], mixT[0:NST, cb * P:(cb + 1) * P], ekbf[0:NST, :], start=True, stop=True)
        st_kl = ST.inc(te.matmul(pst[1][:, 0:P], mixT[0:NST, cb * P:(cb + 1) * P], cumebf[0:NST, :], start=True, stop=True))
        ve.wait_ge(ST.h, st_kl)
        if cb > 0:
            ve.wait_ge(qKS.h, sq_ks[-1])
        rev_dst = AP(ks_st, 127, [[256, P], [-1, P]])
        ve.tensor_scalar(rev_dst, pst[0][:, 0:P], w1_pb[:, cb:cb + 1], None, ALU.mult)
        ve.tensor_scalar(ks_st[:, 127:128], ks_st[:, 127:128], dw1_pb[:, cb:cb + 1], None, ALU.add)
        SV.inc(ve.tensor_scalar(bpst[:, cb * P:(cb + 1) * P], pst[1][:, 0:P], d_pb[:, cb:cb + 1], b1_pb[:, cb:cb + 1], ALU.add, ALU.mult))
        sv_evac = vfence()
        sy.wait_ge(SV.h, sv_evac)
        sq_ks.append(qKS.inc(sy.dma_start(out=ksrc_hbm[cb * P:(cb + 1) * P, :], in_=ks_st[:]), 16))
    sq_bpw = qBP.inc(sy.dma_start(
        out=AP(bpre_cm.tensor, 0, [[P, P], [P * P, KB], [1, P]]),
        in_=AP(bpst, 0, [[KB * P, P], [P, KB], [1, P]])), 16)
    sc.wait_ge(qBP.h, sq_bpw)
    aq_bpre = qBPT.inc(sc.dma_start(out=bpre[:], in_=bpre_cm, transpose=True), 16)

    sc.wait_ge(qFST.h, sq_fst)
    for cb in range(KB):
        ssab = SS.inc(sc.activation(junk[:], ffst[:, cb * C:(cb + 1) * C], AF.Abs, accum_out=abs8[:, cb:cb + 1]))
    ss_abs = SS.v
    ve.wait_ge(SS.h, ss_abs)
    SV.inc(ve.reduce_sum(r2[:], abs8[:], axis=mybir.AxisListType.X))
    sv_abs = vfence()
    sy.wait_ge(SV.h, sv_abs)
    sq_abw = qSH.inc(sy.dma_start(out=cc_gin[:, :], in_=r2[:]), 16)
    sy.wait_ge(qSH.h, sq_abw)
    sq_abr = qSH.inc(sy.dma_start(out=srow[:], in_=cc_gin), 16)
    ve.wait_ge(qSH.h, sq_abr)
    svt = SV.inc(ve.reduce_sum(sc1[:, 0:1], srow[:], axis=mybir.AxisListType.X))
    ve.wait_ge(SV.h, svt)
    svt = SV.inc(ve.tensor_scalar(sc1[:, 7:8], sc1[:, 0:1], 1.0 / (C * C), None, ALU.mult))
    ve.wait_ge(SV.h, svt)
    svt = SV.inc(ve.tensor_scalar(sc1[:, 1:2], sc1[:, 7:8], 1.0, FFN_EPS, ALU.mult, ALU.add))
    ve.wait_ge(SV.h, svt)
    SV.inc(ve.reciprocal(sc1[:, 2:3], sc1[:, 1:2]))
    sv_beta = vfence()
    sy.wait_ge(SV.h, sv_beta)
    sq_bw = qSH.inc(sy.dma_start(out=scal_hbm[:, 1:3], in_=sc1[:, 1:3]), 16)
    sy.wait_ge(qSH.h, sq_bw)
    sq_br = qSH.inc(sy.dma_start(out=rmx[:], in_=AP(scal_hbm.tensor, 2, [[0, P], [1, 1]])), 16)
    ve.wait_ge(qSH.h, sq_br)
    sq_wq = qWQ.v
    for cb in range(KB):
        if cb > 0:
            ve.wait_ge(qWQ.h, sq_wq)
        ve.tensor_scalar(junk[:], ffst[:, cb * C:(cb + 1) * C], rmx[:], MAGIC, ALU.mult, ALU.add)
        ve.tensor_scalar(junk[:], junk[:], MAGIC, 1.0, ALU.subtract, ALU.min)
        SV.inc(ve.tensor_scalar(wqst[:], junk[:], -1.0, None, ALU.max))
        sv_w = vfence()
        sy.wait_ge(SV.h, sv_w)
        sq_wq = qWQ.inc(sy.dma_start(out=wq_hbm[:, cb * C:(cb + 1) * C], in_=wqst[:]), 16)
    sv_p0_v = SV.v; ss_p0_s = SS.v; st_p0 = ST.v
    if "ksrc" in dbg_o:
        sy.wait_ge(qKS.h, sq_ks[-1])
        qDBG.inc(sy.dma_start(out=dbg_o["ksrc"], in_=ksrc_hbm), 16)
    if "wq" in dbg_o:
        sy.wait_ge(qWQ.h, sq_wq)
        qDBG.inc(sy.dma_start(out=dbg_o["wq"], in_=wq_hbm), 16)
    if "bpre" in dbg_o:
        sy.wait_ge(qBPT.h, aq_bpre)
        qDBG.inc(sy.dma_start(out=dbg_o["bpre"], in_=bpre[:]), 16)

    sy.wait_ge(SV.h, sv_p0_v)
    sy.wait_ge(SS.h, ss_p0_s)
    sy.wait_ge(qWQ.h, sq_wq)
    sy.wait_ge(qBPT.h, aq_bpre)
    sy.wait_ge(qKS.h, qKS.v)
    sy.wait_ge(qBP.h, qBP.v)
    sy.wait_ge(qPB.h, qPB.v)
    sy.wait_ge(ST.h, ST.v)
    sq_x = {}; ss_xn = {}; sv_r2 = {}; sq_xn = {}; sq_r2 = {}
    GW = 4

    def rsig_group(mvg, eps_ap, rsgg, nmg):
        var4 = AP(mvg, 1, [[8, P], [2, GW]])
        mu4 = AP(mvg, 0, [[8, P], [2, GW]])
        ss0 = SS.inc(sc.activation(rt4[:, 0:GW], var4, AF.Ln, bias=eps_ap))
        sc.wait_ge(SS.h, ss0)
        ss1 = SS.inc(sc.activation(rsgg[:, 0:GW], rt4[:, 0:GW], AF.Exp, scale=-0.5))
        ve.wait_ge(SS.h, ss1)
        return SV.inc(ve.scalar_tensor_tensor(nmg[:, 0:GW], mu4, -1.0, rsgg[:, 0:GW], ALU.mult, ALU.mult))

    for g in range(NT // GW):
        tiles = range(g * GW, (g + 1) * GW)
        for i in tiles:
            sl = i % GW
            if i >= GW:
                sy.wait_ge(qSPX[sl].h, qSPX[sl].v)
                sy.wait_ge(SV.h, sv_r2[i - GW])
            sq_x[i] = qLD[sl].inc(sy.dma_start(out=xbuf[sl][:], in_=x_in[i * P:(i + 1) * P, :]), 16)
        for i in tiles:
            sl = i % GW
            ve.wait_ge(qLD[sl].h, sq_x[i])
            SV.inc(ve.bn_stats(stA_g[:, sl * 12:sl * 12 + 6], xbuf[sl][:, 0:512]))
            svb = SV.inc(ve.bn_stats(stA_g[:, sl * 12 + 6:sl * 12 + 12], xbuf[sl][:, 512:1024]))
            ve.wait_ge(SV.h, svb)
            sq_x[(i, "a0")] = SV.inc(ve.bn_aggr(mv0g[:, 2 * sl:2 * sl + 2], stA_g[:, sl * 12:sl * 12 + 12]))
        sc.wait_ge(SV.h, sq_x[(tiles[-1], "a0")])
        sv_nm0 = rsig_group(mv0g, ceps5[:], rsg0g, nm0g)
        sc.wait_ge(SV.h, sv_nm0)
        for i in tiles:
            sl = i % GW
            ss_xn[i] = SS.inc(sc.activation(xbuf[sl][:], xbuf[sl][:], AF.Identity,
                                            scale=rsg0g[:, sl:sl + 1], bias=nm0g[:, sl:sl + 1]))
        for i in tiles:
            sl = i % GW
            ve.wait_ge(SS.h, ss_xn[i])
            SV.inc(ve.bn_stats(stA_g[:, sl * 12:sl * 12 + 6], xbuf[sl][:, 0:512]))
            svb = SV.inc(ve.bn_stats(stA_g[:, sl * 12 + 6:sl * 12 + 12], xbuf[sl][:, 512:1024]))
            ve.wait_ge(SV.h, svb)
            sq_x[(i, "a1")] = SV.inc(ve.bn_aggr(mv1g[:, 2 * sl:2 * sl + 2], stA_g[:, sl * 12:sl * 12 + 12]))
        sc.wait_ge(SV.h, sq_x[(tiles[-1], "a1")])
        rsig_group(mv1g, ceps5[:], rsg1g, nm1g)
        sv_nm2 = rsig_group(mv1g, ceps3[:], rsg2g, nm2g)
        ve.wait_ge(SV.h, sv_nm2)
        for i in tiles:
            sl = i % GW
            if i >= GW:
                ve.wait_ge(qSPR[sl].h, qSPR[sl].v)
            ve.tensor_scalar(rawbig[:, i * C:(i + 1) * C], xbuf[sl][:], rsg1g[:, sl:sl + 1], nm1g[:, sl:sl + 1], ALU.mult, ALU.add)
            sv_r2[i] = SV.inc(ve.tensor_scalar(r2buf[sl][:], xbuf[sl][:], rsg2g[:, sl:sl + 1], nm2g[:, sl:sl + 1], ALU.mult, ALU.add))
        svf = vfence()
        for i in tiles:
            sv_r2[i] = svf
        for i in tiles:
            sl = i % GW
            sy.wait_ge(SV.h, sv_r2[i])
            sq_xn[i] = qSPX[sl].inc(sy.dma_start(out=xn_sp[i * P:(i + 1) * P, :], in_=xbuf[sl][:]), 16)
            sq_r2[i] = qSPR[sl].inc(sy.dma_start(out=raw2_sp[i * P:(i + 1) * P, :], in_=r2buf[sl][:]), 16)

    if "xn" in dbg_o:
        for k in range(4):
            sy.wait_ge(qSPX[k].h, qSPX[k].v)
        qDBG.inc(sy.dma_start(out=dbg_o["xn"], in_=xn_sp), 16)
    if "raw2" in dbg_o:
        for k in range(4):
            sy.wait_ge(qSPR[k].h, qSPR[k].v)
        qDBG.inc(sy.dma_start(out=dbg_o["raw2"], in_=raw2_sp), 16)

    st_agg = {}; sv_s = {}
    for j in range(NT):
        te.wait_ge(SV.h, sv_r2[j])
        if j >= 2:
            te.wait_ge(SV.h, sv_s[j - 2])
        pa = psA if j % 2 == 0 else psB
        rb = rawbig[:, j * C:(j + 1) * C]
        te.matmul(pa[0:NST, 0:512], einT[:], rb[:, 0:512], start=True, stop=True)
        st_agg[j] = ST.inc(te.matmul(pa[NST:P, 0:512], einT[:], rb[:, 512:1024], start=True, stop=True, tile_position=(0, NST)))
        ve.wait_ge(ST.h, st_agg[j])
        ve.tensor_mul(smbig[:, j * 512:(j + 1) * 512], s_raw[:], mixpT_sm[:])
        sv_s[j] = SV.inc(ve.scalar_tensor_tensor(s_raw[:], s_raw[:], dec[:], pa[:, 0:512], ALU.mult, ALU.add))
    if "sm" in dbg_o:
        sy.wait_ge(SV.h, sv_s[NT - 1])
        qDBG.inc(sy.dma_start(out=dbg_o["sm"], in_=smbig[:]), 16)

    ve.tensor_mul(s_tmp[:], s_raw[:], w1sm[:])
    ve.tensor_add(s_tmp[:], s_tmp[:], b1g[:])
    SV.inc(ve.tensor_scalar(s_tmp[:], s_tmp[:], emask[:], None, ALU.mult))
    sv_send = vfence()
    sy.wait_ge(SV.h, sv_send)
    sq_ccw = qSH.inc(sy.dma_start(out=cc_sin, in_=s_tmp[:]), 16)
    gp.wait_ge(qSH.h, sq_ccw)
    cc1 = CCS.inc(gp.collective_compute(
        "AllReduce", ALU.add, replica_groups=[[0, 1], [2, 3], [4, 5], [6, 7]],
        ins=[cc_sin], outs=[cc_sout]))
    sy.wait_ge(CCS.h, cc1)
    sq_ccr = qSH.inc(sy.dma_start(out=s_tmp[:], in_=cc_sout), 16)
    ve.wait_ge(qSH.h, sq_ccr)
    ve.tensor_scalar(s_tmp[:], s_tmp[:], omask[:], None, ALU.mult)
    SV.inc(ve.tensor_mul(zqb[:], s_tmp[:], mixT_sm[:]))
    sv_zqb = vfence()

    te.wait_ge(SV.h, sv_zqb)
    te.wait_ge(qBPT.h, aq_bpre)
    sv_yi = {}
    ss_aj = {}; sv_agj = {}
    for j in range(NT):
        pb2 = j % 2
        sc.wait_ge(SV.h, sv_zqb if j < 2 else sv_yi[j - 2])
        ss_aj[j] = SS.inc(sc.activation(ajb[pb2][:], io2_bc[:], AF.Exp, scale=negp[:], bias=npj[:, j:j + 1]))
        ve.wait_ge(SS.h, ss_aj[j])
        SV.inc(ve.tensor_scalar(agjb[pb2][:], aoutD[:], geoj[:, j:j + 1], None, ALU.mult))
        sv_agj[j] = vfence()
        te.wait_ge(SV.h, sv_agj[j])
        if j >= 2:
            te.wait_ge(SV.h, sv_yi[j - 2])
        pc = psc[pb2]
        smj = smbig[:, j * 512:(j + 1) * 512]
        te.matmul(pc[:, 0:512], aoutD[0:NST, :], smj[0:NST, :], start=True, stop=False)
        te.matmul(pc[:, 0:512], agjb[pb2][0:NST, :], bm1[0:NST, :], start=False, stop=False)
        te.matmul(pc[:, 0:512], ajb[pb2][0:NST, :], zqb[0:NST, :], start=False, stop=True)
        te.matmul(pc[:, 512:1024], aoutD[NST:P, :], smj[NST:P, :], start=True, stop=False, tile_position=(NST, 0))
        te.matmul(pc[:, 512:1024], agjb[pb2][NST:P, :], bm1[NST:P, :], start=False, stop=False, tile_position=(NST, 0))
        st_c = ST.inc(te.matmul(pc[:, 512:1024], ajb[pb2][NST:P, :], zqb[NST:P, :], start=False, stop=True, tile_position=(NST, 0)))
        ve.wait_ge(ST.h, st_c)
        yi_dst = AP(yibig, j, [[NT * C, P], [32, C]])
        sv_yi[j] = SV.inc(ve.tensor_tensor(yi_dst, pc[:], bpre[:], ALU.add))
    sv_yi_all = SV.v

    NGRP = 32
    gq_h = {}; sv_rev = {}; st_tp = {}; sv_tev = {}
    gp.wait_ge(qKS.h, sq_ks[-1])
    gp.wait_ge(ST.h, ST.v)
    gp.wait_ge(SV.h, sv_yi_all)
    if "sm" in dbg_o:
        gp.wait_ge(qDBG.h, qDBG.v)
    for g in range(NGRP):
        hb2 = g % 2
        if g >= 2:
            gp.wait_ge(SV.h, sv_rev[g - 2])
        src = AP(ksrc_hbm.tensor, g * 32 * 256, [[1, P], [256, 32], [1, P]])
        gq_h[g] = qHK[hb2].inc(gp.dma_start(out=hank[hb2][:].rearrange("p (c j) -> p c j", c=32), in_=src), 16)
        ve.wait_ge(qHK[hb2].h, gq_h[g])
        if g >= 2:
            ve.wait_ge(ST.h, st_tp[g - 2])
        rev_src = AP(hank[hb2], 127, [[32 * P, P], [P, 32], [-1, P]])
        SV.inc(ve.tensor_copy(wtoep[hb2][:].rearrange("p (c j) -> p c j", c=32), rev_src))
        sv_rev[g] = vfence()
        te.wait_ge(SV.h, sv_rev[g])
        if g >= 2:
            te.wait_ge(SV.h, sv_tev[g - 2])
        tp_pair = (pst[0], pst[1]) if g % 2 == 0 else (psA, psB)
        mm = None
        for cl in range(32):
            pt = tp_pair[cl // 16]
            mov = AP(rawbig, g * 32 + cl, [[NT * C, P], [C, NT]])
            mm = te.matmul(pt[:, (cl % 16) * 32:(cl % 16) * 32 + 32], wtoep[hb2][:, cl * P:(cl + 1) * P], mov, start=True, stop=True)
        st_tp[g] = ST.inc(mm)
        ve.wait_ge(ST.h, st_tp[g])
        ve.tensor_tensor(yibig[:, g * 1024:g * 1024 + 512], yibig[:, g * 1024:g * 1024 + 512], tp_pair[0][:, 0:512], ALU.add)
        SV.inc(ve.tensor_tensor(yibig[:, g * 1024 + 512:(g + 1) * 1024], yibig[:, g * 1024 + 512:(g + 1) * 1024], tp_pair[1][:, 0:512], ALU.add))
        sv_tev[g] = vfence()
    sv_tev_all = SV.v
    if "yi" in dbg_o:
        sy.wait_ge(SV.h, sv_tev_all)
        qDBG.inc(sy.dma_start(out=dbg_o["yi"], in_=yibig[:]), 16)
    if "raw1" in dbg_o:
        sy.wait_ge(ST.h, st_tp[NGRP - 1])
        qDBG.inc(sy.dma_start(out=AP(dbg_o["raw1"].tensor, 0, [[C, P], [C * P, NT], [1, C]]),
                            in_=AP(rawbig, 0, [[NT * C, P], [C, NT], [1, C]])), 16)

    sy.wait_ge(SV.h, sv_r2[NT - 1])
    for k in range(4):
        sy.wait_ge(qSPX[k].h, qSPX[k].v)
    sq_xnr = {}; sg_xat = {}; ss_ln2 = {}; sq_xat = {}; sq_ln2 = {}
    sv_a3 = {}
    sc.wait_ge(ST.h, st_tp[NGRP - 1])
    sc.wait_ge(qCAST.h, gq_casts)
    aq_w1t = {}
    for kb in range(KB):
        aq_w1t[kb] = qW1T.inc(sc.dma_start(out=fc1wT[:, kb * HD:(kb + 1) * HD], in_=fc1w_bf[:, kb * P:(kb + 1) * P], transpose=True), 16)
    for g in range(NT // GW):
        tiles = range(g * GW, (g + 1) * GW)
        for i in tiles:
            sl = i % GW
            if i >= GW:
                sy.wait_ge(qSPA[sl].h, qSPA[sl].v)
                sy.wait_ge(SS.h, ss_ln2[i - GW])
            sq_xnr[i] = qLD2[sl].inc(sy.dma_start(out=xnre[sl][:], in_=xn_sp[i * P:(i + 1) * P, :]), 16)
        for i in tiles:
            sl = i % GW
            gp.wait_ge(qLD2[sl].h, sq_xnr[i])
            if i == 0:
                gp.wait_ge(SV.h, sv_tev_all)
            yi_view = AP(yibig, i, [[NT * C, P], [32, C]])
            sg_xat[i] = SG.inc(gp.tensor_tensor(xnre[sl][:], xnre[sl][:], yi_view, ALU.add))
        for i in tiles:
            sl = i % GW
            ve.wait_ge(SG.h, sg_xat[i])
            SV.inc(ve.bn_stats(stA_g[:, sl * 12:sl * 12 + 6], xnre[sl][:, 0:512]))
            svb = SV.inc(ve.bn_stats(stA_g[:, sl * 12 + 6:sl * 12 + 12], xnre[sl][:, 512:1024]))
            ve.wait_ge(SV.h, svb)
            sv_a3[i] = SV.inc(ve.bn_aggr(mv0g[:, 2 * sl:2 * sl + 2], stA_g[:, sl * 12:sl * 12 + 12]))
        sc.wait_ge(SV.h, sv_a3[tiles[-1]])
        sv_nm3 = rsig_group(mv0g, ceps5[:], rsg3g, nm3g)
        sc.wait_ge(SV.h, sv_nm3)
        for i in tiles:
            sl = i % GW
            if i >= GW:
                sc.wait_ge(qSPL[sl].h, qSPL[sl].v)
            ss_ln2[i] = SS.inc(sc.activation(ln2b[sl][:], xnre[sl][:], AF.Identity,
                                             scale=rsg3g[:, sl:sl + 1], bias=nm3g[:, sl:sl + 1]))
        for i in tiles:
            sl = i % GW
            sy.wait_ge(SS.h, ss_ln2[i])
            sq_xat[i] = qSPA[sl].inc(sy.dma_start(out=xattn_sp[i * P:(i + 1) * P, :], in_=xnre[sl][:]), 16)
            sq_ln2[i] = qSPL[sl].inc(sy.dma_start(out=ln2_sp[i * P:(i + 1) * P, :], in_=ln2b[sl][:]), 16)
    if "xattn" in dbg_o:
        for k in range(4):
            sy.wait_ge(qSPA[k].h, qSPA[k].v)
        qDBG.inc(sy.dma_start(out=dbg_o["xattn"], in_=xattn_sp), 16)
    if "ln2" in dbg_o:
        for k in range(4):
            sy.wait_ge(qSPL[k].h, qSPL[k].v)
        qDBG.inc(sy.dma_start(out=dbg_o["ln2"], in_=ln2_sp), 16)

    sc.wait_ge(SG.h, sg_xat[NT - 1])
    aq_w2t = {}
    for hb in range(HB):
        aq_w2t[hb] = qW2T.inc(sc.dma_start(out=fc2wT[:, hb * C:(hb + 1) * C], in_=fc2w_bf[:, hb * P:(hb + 1) * P], transpose=True), 16)
    sv_n2bf = SV.inc(ve.tensor_copy(n2b_bf[:], n2b_pb[:, 0:KB]))
    te.wait_ge(SV.h, sv_n2bf)
    te.wait_ge(SV.h, sv_yi_all)
    te.wait_ge(qW1T.h, aq_w1t[KB - 1])
    for hb in range(HB):
        for kb in range(KB):
            ins_mm = te.matmul(psc[0][:, hb:hb + 1], fc1wT[:, kb * HD + hb * P:kb * HD + (hb + 1) * P],
                               n2b_bf[:, kb:kb + 1], start=(kb == 0), stop=(kb == KB - 1))
    st_bf = ST.inc(ins_mm)
    ve.wait_ge(ST.h, st_bf)
    SV.inc(ve.tensor_add(b1p[:, 0:HB], psc[0][:, 0:HB], fc1b_pb[:, 0:HB]))
    sv_b1p = vfence()
    sv_w1s = {}
    for kb in range(KB):
        sv_w1s[kb] = SV.inc(ve.tensor_scalar(fc1wT[:, kb * HD:(kb + 1) * HD], fc1wT[:, kb * HD:(kb + 1) * HD], n2w_pb[:, kb:kb + 1], None, ALU.mult))
    sv_wscale = vfence()

    for k in range(4):
        sc.wait_ge(qSPL[k].h, qSPL[k].v)
        sc.wait_ge(qSPR[k].h, qSPR[k].v)
    sc.wait_ge(SG.h, sg_xat[NT - 1])
    aq_r2t = {}; ss_xn2 = {}; sq_xn2 = {}; sv_mx = {}
    for tt in range(T3):
        b = tt % 2
        for kb in range(KB):
            if tt >= 2:
                sc.wait_ge(SS.h, ss_xn2[tt - 2])
            aq_r2t[(tt, kb)] = qR2T[b].inc(sc.dma_start(
                out=r2T[b][:, kb * NTOK:(kb + 1) * NTOK],
                in_=raw2_sp[tt * NTOK:(tt + 1) * NTOK, kb * P:(kb + 1) * P], transpose=True), 16)
        sc.wait_ge(qR2T[b].h, aq_r2t[(tt, KB - 1)])
        if tt >= 1:
            sc.wait_ge(qX2S.h, sq_xn2[tt - 1])
            sc.wait_ge(SV.h, sv_mx[tt - 1])
        for kb in range(KB):
            ins_aff = sc.activation(xqT[:, kb * NTOK:(kb + 1) * NTOK], r2T[b][:, kb * NTOK:(kb + 1) * NTOK],
                                    AF.Identity, scale=g_pb[:, kb:kb + 1], bias=bta_pb[:, kb:kb + 1])
        ss_xn2[tt] = SS.inc(ins_aff)
        ve.wait_ge(SS.h, ss_xn2[tt])
        svt = SV.inc(ve.reduce_max(rmx[:], xqT[:], axis=mybir.AxisListType.X, apply_absolute_value=True))
        ve.wait_ge(SV.h, svt)
        SV.inc(ve.tensor_max(smax[:], smax[:], rmx[:]))
        sv_mx[tt] = vfence()
        sy.wait_ge(SS.h, ss_xn2[tt])
        sq_xn2[tt] = qX2S.inc(sy.dma_start(
            out=AP(xn2_sp.tensor, tt * NTOK, [[TLOC, P], [P * TLOC, KB], [1, NTOK]]),
            in_=AP(xqT, 0, [[KB * NTOK, P], [NTOK, KB], [1, NTOK]])), 16)
    sy.wait_ge(SV.h, sv_mx[T3 - 1])
    sq_gw = qSH.inc(sy.dma_start(out=cc_gin[:, :], in_=smax[:]), 16)
    gp.wait_ge(qSH.h, sq_gw)
    cc2 = CCS.inc(gp.collective_compute(
        "AllReduce", ALU.max, replica_groups=[list(range(8))],
        ins=[cc_gin], outs=[cc_gout]))
    sy.wait_ge(CCS.h, cc2)
    sq_gr = qSH.inc(sy.dma_start(out=srow[:], in_=cc_gout), 16)
    ve.wait_ge(qSH.h, sq_gr)
    svt = SV.inc(ve.reduce_max(sc1[:, 3:4], srow[:], axis=mybir.AxisListType.X))
    ve.wait_ge(SV.h, svt)
    svt = SV.inc(ve.reciprocal(sc1[:, 4:5], sc1[:, 3:4]))
    ve.wait_ge(SV.h, svt)
    svt = SV.inc(ve.tensor_scalar(sc1[:, 5:6], sc1[:, 4:5], 128.0, None, ALU.mult))
    ve.wait_ge(SV.h, svt)
    svt = SV.inc(ve.tensor_tensor(sc1[:, 6:7], sc1[:, 3:4], sc1[:, 7:8], ALU.mult))
    ve.wait_ge(SV.h, svt)
    SV.inc(ve.tensor_scalar(sc1[:, 6:7], sc1[:, 6:7], 1.0 / 128.0, None, ALU.mult))
    sv_q = vfence()
    sy.wait_ge(SV.h, sv_q)
    sq_qw = qSH.inc(sy.dma_start(out=scal_hbm[:, 4:7], in_=sc1[:, 4:7]), 16)
    sy.wait_ge(qSH.h, sq_qw)
    qSH.inc(sy.dma_start(out=rqbc[:], in_=AP(scal_hbm.tensor, 5, [[0, P], [1, 1]])), 16)
    sq_ysc = qSH.inc(sy.dma_start(out=ysc[:], in_=AP(scal_hbm.tensor, 6, [[0, P], [1, 1]])), 16)
    if "scal" in dbg_o:
        sq_sc16 = qSH.inc(sy.dma_start(out=scal_hbm[:, 16:32], in_=sc1[:, 0:16]), 16)
        sy.wait_ge(qSH.h, sq_sc16)
        qDBG.inc(sy.dma_start(out=dbg_o["scal"], in_=scal_hbm), 16)

    sy.wait_ge(qWQ.h, sq_wq)
    sq_wql = qWQL.inc(sy.dma_start(out=wq_sb[:], in_=wq_hbm), 16)
    st_ffn = {}; ss_g = {}; sq_g = {}
    sq_x2l = {}
    sy.wait_ge(SS.h, ss_xn2[T3 - 1])
    sy.wait_ge(qX2S.h, qX2S.v)
    for tt in range(T3):
        b = tt % 2
        if tt >= 2:
            sy.wait_ge(SV.h, sv_xq[tt - 2])
        sq_x2l[tt] = qX2L[b].inc(sy.dma_start(
            out=AP(r2T[b], 0, [[KB * NTOK, P], [NTOK, KB], [1, NTOK]]),
            in_=AP(xn2_sp.tensor, tt * NTOK, [[TLOC, P], [P * TLOC, KB], [1, NTOK]])), 16)
        ve.wait_ge(qX2L[b].h, sq_x2l[tt])
        ve.wait_ge(qSH.h, sq_ysc)
        if tt >= 1:
            ve.wait_ge(ST.h, st_ffn[(tt - 1, KB - 1)])
        ve.tensor_scalar(xqT[:], r2T[b][:], rqbc[:], 128.0 - FFN_EPS, ALU.mult, ALU.min)
        if tt == 0:
            sv_xq = {}
        SV.inc(ve.tensor_scalar(xqT[:], xqT[:], -(128.0 - FFN_EPS), None, ALU.max))
        sv_xq[tt] = vfence()
        te.wait_ge(SV.h, sv_xq[tt])
        te.wait_ge(qWQL.h, sq_wql)
        for ub in range(KB):
            pf = pst[ub % 2]
            prev = (tt, ub - 2) if ub >= 2 else ((tt - 1, KB - 2 + ub) if tt > 0 else None)
            if prev is not None:
                te.wait_ge(SS.h, ss_g[prev])
            for kb in range(KB):
                mm = te.matmul(pf[:, 0:NTOK], wq_sb[:, kb * C + ub * P:kb * C + (ub + 1) * P],
                               xqT[:, kb * NTOK:(kb + 1) * NTOK], start=(kb == 0), stop=(kb == KB - 1))
            st_ffn[(tt, ub)] = ST.inc(mm)
            sc.wait_ge(ST.h, st_ffn[(tt, ub)])
            sc.wait_ge(qSH.h, sq_ysc)
            if prev is not None:
                sc.wait_ge(qG[prev[1] % 2].h, sq_g[prev])
            ss_g[(tt, ub)] = SS.inc(sc.activation(gst[ub % 2][:], pf[:, 0:NTOK], AF.Silu, scale=ysc[:]))
            sy.wait_ge(SS.h, ss_g[(tt, ub)])
            sq_g[(tt, ub)] = qG[ub % 2].inc(sy.dma_start(
                out=AP(g_sp.tensor, ub * P * TLOC + tt * NTOK, [[TLOC, P], [1, NTOK]]),
                in_=gst[ub % 2][:]), 16)
    sq_g_all = (qG[0].v, qG[1].v)
    if "xn2" in dbg_o:
        qDBG.inc(sy.dma_start(out=dbg_o["xn2"], in_=xn2_sp), 16)
    if "gsp" in dbg_o:
        qDBG.inc(sy.dma_start(out=dbg_o["gsp"], in_=g_sp), 16)

    last_ffn = st_ffn[(T3 - 1, KB - 1)]
    sc.wait_ge(ST.h, last_ffn)
    sc.wait_ge(qG[0].h, sq_g_all[0])
    sc.wait_ge(qG[1].h, sq_g_all[1])
    aq_l2 = {}; st_f1 = {}; ss_ge = {}; st_f2 = {}; ss_m = {}; sq_m = {}
    sv_cmb = {}; sq_out = {}; sq_xa3 = {}; sq_x3 = {}; aq_mt = {}; aq_gt = {}
    te.wait_ge(SS.h, ss_g[(T3 - 1, KB - 1)])
    te.wait_ge(SV.h, sv_tev_all)
    te.wait_ge(SV.h, sv_b1p)
    def ln2t_load(tt):
        for kb in range(KB):
            if tt > 0:
                sc.wait_ge(ST.h, st_f1[(tt - 1, HB - 1)])
            aq_l2[(tt, kb)] = qL2T.inc(sc.dma_start(
                out=ln2T[:, kb * NTOK:(kb + 1) * NTOK],
                in_=ln2_sp[tt * NTOK:(tt + 1) * NTOK, kb * P:(kb + 1) * P], transpose=True), 16)
    def combine_tile(tt):
        for rr in range(NTOK // P):
            i = tt * (NTOK // P) + rr
            prev = i - 1
            if prev >= 0:
                sy.wait_ge(SV.h, sv_cmb[prev])
                sy.wait_ge(qOUT.h, sq_out[prev])
                sc.wait_ge(SV.h, sv_cmb[prev])
            if i == 0:
                for k in range(4):
                    sy.wait_ge(qSPA[k].h, qSPA[k].v)
            sq_xa3[i] = qXA.inc(sy.dma_start(out=xatr[:], in_=xattn_sp[i * P:(i + 1) * P, :]), 16)
            sq_x3[i] = qXR.inc(sy.dma_start(out=xre[:], in_=x_in[i * P:(i + 1) * P, :]), 16)
            sc.wait_ge(qM[0].h, qM[0].v)
            sc.wait_ge(qM[1].h, qM[1].v)
            aq_mt[i] = qMT.inc(sc.dma_start(out=mlpT[:], in_=mlp_sp[:, i * P:(i + 1) * P], transpose=True), 16)
            aq_gt[i] = qGT.inc(sc.dma_start(out=gT[:], in_=g_sp[:, i * P:(i + 1) * P], transpose=True), 16)
            ve.wait_ge(qXA.h, sq_xa3[i])
            ve.wait_ge(qXR.h, sq_x3[i])
            ve.wait_ge(qMT.h, aq_mt[i])
            ve.wait_ge(qGT.h, aq_gt[i])
            if prev >= 0:
                ve.wait_ge(qOUT.h, sq_out[prev])
            ve.tensor_add(outb[:], xatr[:], mlpT[:])
            ve.tensor_mul(outb[:], outb[:], gT[:])
            SV.inc(ve.tensor_add(outb[:], outb[:], xre[:]))
            sv_cmb[i] = vfence()
            sy.wait_ge(SV.h, sv_cmb[i])
            sq_out[i] = qOUT.inc(sy.dma_start(out=out_hbm[i * P:(i + 1) * P, :], in_=outb[:]), 16)
    ln2t_load(0)
    for tt in range(T3):
        te.wait_ge(qL2T.h, aq_l2[(tt, KB - 1)])
        if tt == 0:
            te.wait_ge(SV.h, sv_wscale)
            te.wait_ge(qW2T.h, aq_w2t[HB - 1])
        for mb in range(HB):
            if mb % 4 < 2:
                pfs = pst[mb % 2][:, 0:NTOK]
            else:
                pfs = psc[0][:, (mb % 4 - 2) * NTOK:(mb % 4 - 1) * NTOK]
            prev = (tt, mb - 4) if mb >= 4 else ((tt - 1, HB - 4 + mb) if tt > 0 else None)
            if prev is not None:
                te.wait_ge(SS.h, ss_ge[prev])
            for kb in range(KB):
                mm = te.matmul(pfs, fc1wT[:, kb * HD + mb * P:kb * HD + (mb + 1) * P],
                               ln2T[:, kb * NTOK:(kb + 1) * NTOK], start=(kb == 0), stop=(kb == KB - 1))
            st_f1[(tt, mb)] = ST.inc(mm)
            sc.wait_ge(ST.h, st_f1[(tt, mb)])
            if tt > 0:
                sc.wait_ge(ST.h, st_f2[(tt - 1, KB - 1)])
            ss_ge[(tt, mb)] = SS.inc(sc.activation(gbig[:, mb * NTOK:(mb + 1) * NTOK], pfs,
                                                   AF.Gelu, bias=b1p[:, mb:mb + 1]))
        if tt + 1 < T3:
            ln2t_load(tt + 1)
        if tt > 0:
            combine_tile(tt - 1)
        te.wait_ge(SS.h, ss_ge[(tt, HB - 1)])
        for cb in range(KB):
            pg = psA if cb % 2 == 0 else psB
            prev = (tt, cb - 2) if cb >= 2 else ((tt - 1, KB - 2 + cb) if tt > 0 else None)
            if prev is not None:
                te.wait_ge(SS.h, ss_m[prev])
            for hb in range(HB):
                mm = te.matmul(pg[:, 0:NTOK], fc2wT[:, hb * C + cb * P:hb * C + (cb + 1) * P],
                               gbig[:, hb * NTOK:(hb + 1) * NTOK], start=(hb == 0), stop=(hb == HB - 1))
            st_f2[(tt, cb)] = ST.inc(mm)
            sc.wait_ge(ST.h, st_f2[(tt, cb)])
            if prev is not None:
                sc.wait_ge(qM[prev[1] % 2].h, sq_m[prev])
            ss_m[(tt, cb)] = SS.inc(sc.activation(mst[cb % 2][:], pg[:, 0:NTOK], AF.Identity, bias=fc2b_pb[:, cb:cb + 1]))
            sy.wait_ge(SS.h, ss_m[(tt, cb)])
            sq_m[(tt, cb)] = qM[cb % 2].inc(sy.dma_start(
                out=AP(mlp_sp.tensor, cb * P * TLOC + tt * NTOK, [[TLOC, P], [1, NTOK]]),
                in_=mst[cb % 2][:]), 16)
    for tt in [T3 - 1]:
        combine_tile(tt)
    if "mlp" in dbg_o:
        sy.wait_ge(qM[0].h, qM[0].v)
        sy.wait_ge(qM[1].h, qM[1].v)
        qDBG.inc(sy.dma_start(out=dbg_o["mlp"], in_=mlp_sp), 16)

    for q in ALLQ:
        if q.v:
            sy.wait_ge(q.h, q.v)

    ctx.close()
    return nc


def make_in_maps(inputs):
    x = np.ascontiguousarray(np.asarray(inputs["L0_attn_x"], dtype=np.float32))
    def f32(k):
        return np.ascontiguousarray(np.asarray(inputs[k], dtype=np.float32))
    base = {
        "log_tau": f32("L0_attn_log_tau").reshape(NST, 1),
        "log_dt": f32("L0_attn_log_dt").reshape(1, 1),
        "mix": f32("L0_attn_mix"),
        "dvec": f32("L0_attn_D").reshape(1, C),
        "n1w": f32("L0_attn_norm1_w").reshape(1, C),
        "n1b": f32("L0_attn_norm1_b").reshape(1, C),
        "n2w": f32("L0_attn_norm2_w").reshape(1, C),
        "n2b": f32("L0_attn_norm2_b").reshape(1, C),
        "fc1w": f32("L0_attn_fc1_w"),
        "fc1b": f32("L0_attn_fc1_b").reshape(1, HD),
        "fc2w": f32("L0_attn_fc2_w"),
        "fc2b": f32("L0_attn_fc2_b").reshape(1, C),
        "ffnw": f32("L0_ffn_weight"),
        "lng": f32("L0_ffn_ln_gamma").reshape(1, C),
        "lnb": f32("L0_ffn_ln_beta").reshape(1, C),
        "iota128": np.arange(P, dtype=np.float32).reshape(1, P),
        "iota128p1": (np.arange(P, dtype=np.float32) + 1).reshape(1, P),
        "rev127": (127.0 - np.arange(P, dtype=np.float32)).reshape(P, 1),
        "j128": (128.0 * np.arange(NT + 1, dtype=np.float32)).reshape(1, NT + 1),
    }
    in_maps = []
    for core in range(8):
        b, hf = core // 2, core % 2
        m = dict(base)
        m["x_sh"] = np.ascontiguousarray(x[b, hf * TLOC:(hf + 1) * TLOC, :])
        m["emask"] = np.array([[1.0 if hf == 0 else 0.0]], dtype=np.float32)
        m["omask"] = np.array([[1.0 if hf == 1 else 0.0]], dtype=np.float32)
        in_maps.append(m)
    return in_maps


def assemble(results):
    out = np.empty((4, 2 * TLOC, C), dtype=np.float32)
    for core in range(8):
        b, hf = core // 2, core % 2
        out[b, hf * TLOC:(hf + 1) * TLOC, :] = results[core]["out_sh"]
    return out


if __name__ == "__main__":
    nc = build()
    print("built, instructions:", nc.n_instructions())


_NC_CACHE = {}


def kernel(**inputs):
    from concourse.bass_utils import run_bass_kernel_spmd
    if "nc" not in _NC_CACHE:
        _NC_CACHE["nc"] = build()
    nc = _NC_CACHE["nc"]
    in_maps = make_in_maps(inputs)
    res = run_bass_kernel_spmd(nc, in_maps, core_ids=list(range(8)))
    return assemble(res.results)


# version 07
# speedup vs baseline: 1.0969x; 1.0351x over previous
import numpy as np
import concourse.bass as bass
import concourse.mybir as mybir
from concourse.bass_types import AP
from contextlib import ExitStack

F32 = mybir.dt.float32
BF16 = mybir.dt.bfloat16
AF = mybir.ActivationFunctionType
ALU = mybir.AluOpType

P = 128
C = 1024
HD = 4096
TLOC = 4096
NT = 32
KB = 8
HB = 32
NTOK = 512
T3 = 8
NST = 64
LN0_EPS = 1e-5
LN1_EPS = 1e-5
LNF_EPS = 1e-3
LN2_EPS = 1e-5
FFN_EPS = 1e-5
MAGIC = 12582912.0
SB_TOP = 229344


def build(debug=()):
    nc = bass.Bass("TRN2", target_bir_lowering=False, detect_race_conditions=False)
    dbg = set(debug)

    def din(name, shape, dt=F32):
        return nc.dram_tensor(name, shape, dt, kind="ExternalInput").ap()

    def dint(name, shape, dt=F32):
        return nc.dram_tensor(name, shape, dt, kind="Internal").ap()

    x_in = din("x_sh", [TLOC, C])
    log_tau = din("log_tau", [NST, 1])
    log_dt = din("log_dt", [1, 1])
    mix_in = din("mix", [C, NST])
    d_in = din("dvec", [1, C])
    n1w_in = din("n1w", [1, C]); n1b_in = din("n1b", [1, C])
    n2w_in = din("n2w", [1, C]); n2b_in = din("n2b", [1, C])
    fc1w_in = din("fc1w", [HD, C]); fc1b_in = din("fc1b", [1, HD])
    fc2w_in = din("fc2w", [C, HD]); fc2b_in = din("fc2b", [1, C])
    ffnw_in = din("ffnw", [C, C])
    lng_in = din("lng", [1, C]); lnb_in = din("lnb", [1, C])
    emask_in = din("emask", [1, 1])
    omask_in = din("omask", [1, 1])
    iota128 = din("iota128", [1, P])
    iota128p1 = din("iota128p1", [1, P])
    rev127_in = din("rev127", [P, 1])
    j128_in = din("j128", [1, NT + 1])

    out_hbm = nc.dram_tensor("out_sh", [TLOC, C], F32, kind="ExternalOutput").ap()
    dbg_o = {}
    for name, shape, dt in [
        ("xn", [TLOC, C], F32), ("raw1", [TLOC, C], BF16), ("raw2", [TLOC, C], BF16),
        ("yi", [P, NT * C], BF16), ("xattn", [TLOC, C], F32), ("ln2", [TLOC, C], BF16),
        ("gsp", [C, TLOC], BF16), ("mlp", [C, TLOC], BF16), ("xn2", [KB, P * TLOC], BF16),
        ("wq", [P, KB * C], BF16), ("ksrc", [C, 256], BF16), ("bpre", [P, C], BF16),
        ("sm", [P, NT * 512], BF16), ("scal", [1, 64], F32),
    ]:
        if name in dbg:
            dbg_o[name] = nc.dram_tensor("dbg_" + name, shape, dt, kind="ExternalOutput").ap()

    xn_sp = dint("xn_sp", [TLOC, C])
    raw2_sp = dint("raw2_sp", [TLOC, C], BF16)
    xattn_sp = dint("xattn_sp", [TLOC, C])
    ln2_sp = dint("ln2_sp", [TLOC, C], BF16)
    ksrc_hbm = dint("ksrc_hbm", [C, 256], BF16)
    mix_bf = dint("mix_bf", [C, P], BF16)
    mixsm_hbm = dint("mixsm_hbm", [512, P], BF16)
    fc1w_bf = dint("fc1w_bf", [HD, C], BF16)
    fc2w_bf = dint("fc2w_bf", [C, HD], BF16)
    wq_hbm = dint("wq_hbm", [P, KB * C], BF16)
    xn2_sp = dint("xn2_sp", [KB, P * TLOC], BF16)
    g_sp = dint("g_sp", [C, TLOC], BF16)
    mlp_sp = dint("mlp_sp", [C, TLOC], BF16)
    bpre_cm = dint("bpre_cm", [C, P], BF16)
    scal_hbm = dint("scal_hbm", [1, 64])
    pbc_hbm = dint("pbc_hbm", [1, NST])
    cc_gin = dint("cc_gin", [1, P]); cc_gout = dint("cc_gout", [1, P])
    cc_sin = dint("cc_sin", [P, 512]); cc_sout = dint("cc_sout", [P, 512])

    SB_BASE = 18400

    def sb_at(name, shape, dt, offset):
        return nc.alloc_sbuf_tensor_at(name, shape, dt, offset=offset)

    def mk_region(lo, size):
        return [lo, lo + size]

    def alloc(region, name, shape, dt=F32):
        nbytes = int(np.prod(shape[1:])) * mybir.dt.size(dt)
        o = region[0]
        region[0] = (o + nbytes + 31) & ~31
        assert region[0] <= region[1], f"{name}: overflow {region}"
        return sb_at(name, shape, dt, o)

    RP = mk_region(SB_BASE, 20480)
    RS0 = RP[1]
    RS_SZ = 26624
    RC0 = RS0 + RS_SZ
    RC_SZ = 32768
    RA0 = RC0 + RC_SZ
    RB0 = RA0 + 65536
    assert RB0 + 65536 <= SB_TOP, (RB0 + 65536, SB_TOP)

    einT = alloc(RP, "einT", [P, NST], BF16)
    aoutD = alloc(RP, "aoutD", [P, P], BF16)
    mixT_sm = alloc(RP, "mixT_sm", [P, 512], BF16)
    mixpT_sm = alloc(RP, "mixpT_sm", [P, 512], BF16)
    bm1 = alloc(RP, "bm1", [P, 512], BF16)
    zqb = alloc(RP, "zqb", [P, 512], BF16)
    b1g = alloc(RP, "b1g", [P, 512], BF16)
    s_raw = alloc(RP, "s_raw", [P, 512], F32)
    s_tmp = alloc(RP, "s_tmp", [P, 512], F32)
    geoj = alloc(RP, "geoj", [P, NT + 1], F32)
    npj = alloc(RP, "npj", [P, NT + 1], F32)
    bpre = alloc(RP, "bpre", [P, C], BF16)
    w1_pb = alloc(RP, "w1_pb", [P, 32], F32)
    b1_pb = alloc(RP, "b1_pb", [P, 32], F32)
    dw1_pb = alloc(RP, "dw1_pb", [P, KB], F32)
    d_pb = alloc(RP, "d_pb", [P, 32], F32)
    n2w_pb = alloc(RP, "n2w_pb", [P, 32], F32)
    n2b_pb = alloc(RP, "n2b_pb", [P, 32], F32)
    g_pb = alloc(RP, "g_pb", [P, 32], F32)
    bta_pb = alloc(RP, "bta_pb", [P, 32], F32)
    fc1b_pb = alloc(RP, "fc1b_pb", [P, HB], F32)
    fc2b_pb = alloc(RP, "fc2b_pb", [P, 32], F32)
    b1p = alloc(RP, "b1p", [P, HB], F32)
    w1sm = alloc(RP, "w1sm", [P, 512], BF16)
    b1sm = alloc(RP, "b1sm", [P, 512], BF16)
    ltau = alloc(RP, "ltau", [P, 1], F32)
    negp = alloc(RP, "negp", [P, 1], F32)
    tauv = alloc(RP, "tauv", [P, 1], F32)
    dec = alloc(RP, "dec", [P, 1], F32)
    geoL = alloc(RP, "geoL", [P, 1], F32)
    dtbc = alloc(RP, "dtbc", [P, 1], F32)
    emask = alloc(RP, "emask", [P, 1], F32)
    omask = alloc(RP, "omask", [P, 1], F32)
    rqbc = alloc(RP, "rqbc", [P, 1], F32)
    ysc = alloc(RP, "ysc", [P, 1], F32)
    rev127 = alloc(RP, "rev127", [P, 1], F32)
    io2_bc = alloc(RP, "io2_bc", [P, P], F32)
    j128_bc = alloc(RP, "j128_bc", [P, NT + 1], F32)
    prow_bc = alloc(RP, "prow_bc", [P, NST], F32)
    smax = alloc(RP, "smax", [P, 1], F32)
    srow = alloc(RP, "srow", [1, P], F32)
    sc1 = alloc(RP, "sc1", [1, 16], F32)
    ajb = [alloc(RP, f"ajb{k}", [P, P], BF16) for k in range(2)]
    agjb = [alloc(RP, f"agjb{k}", [P, P], BF16) for k in range(2)]
    mv0 = alloc(RP, "mv0", [P, 2], F32)
    mv1 = alloc(RP, "mv1", [P, 2], F32)
    stA = alloc(RP, "stA", [P, 12], F32)
    r1 = alloc(RP, "r1", [P, 1], F32); r2 = alloc(RP, "r2", [P, 1], F32)
    r3 = alloc(RP, "r3", [P, 1], F32); r4 = alloc(RP, "r4", [P, 1], F32)
    r5 = alloc(RP, "r5", [P, 1], F32)
    rsg0 = alloc(RP, "rsg0", [P, 1], F32); nm0 = alloc(RP, "nm0", [P, 1], F32)
    rsg1 = alloc(RP, "rsg1", [P, 1], F32); nm1 = alloc(RP, "nm1", [P, 1], F32)
    rsg2 = alloc(RP, "rsg2", [P, 1], F32); nm2 = alloc(RP, "nm2", [P, 1], F32)
    rsg3 = alloc(RP, "rsg3", [P, 1], F32); nm3 = alloc(RP, "nm3", [P, 1], F32)
    rmx = alloc(RP, "rmx", [P, 1], F32)
    stA_g = alloc(RP, "stA_g", [P, 48], F32)
    mv0g = alloc(RP, "mv0g", [P, 8], F32)
    mv1g = alloc(RP, "mv1g", [P, 8], F32)
    rsg0g = alloc(RP, "rsg0g", [P, 4], F32)
    nm0g = alloc(RP, "nm0g", [P, 4], F32)
    rsg1g = alloc(RP, "rsg1g", [P, 4], F32)
    nm1g = alloc(RP, "nm1g", [P, 4], F32)
    rsg2g = alloc(RP, "rsg2g", [P, 4], F32)
    nm2g = alloc(RP, "nm2g", [P, 4], F32)
    rsg3g = alloc(RP, "rsg3g", [P, 4], F32)
    nm3g = alloc(RP, "nm3g", [P, 4], F32)
    rt4 = alloc(RP, "rt4", [P, 4], F32)
    vd = alloc(RP, "vd", [P, 1], F32)
    ident = alloc(RP, "ident", [P, P], BF16)
    pcol1 = alloc(RP, "pcol1", [P, 1], F32)
    ceps5 = alloc(RP, "ceps5", [P, 1], F32)
    n2b_bf = alloc(RP, "n2b_bf", [P, KB], BF16)
    ceps3 = alloc(RP, "ceps3", [P, 1], F32)

    S0 = mk_region(RS0, RS_SZ)
    ks_st = alloc(S0, "ks_st", [P, 256], BF16)
    mixT = alloc(S0, "mixT", [P, C], BF16)
    io1_bc = alloc(S0, "io1_bc", [P, P], F32)
    ek32 = alloc(S0, "ek32", [P, P], F32)
    ekbf = alloc(S0, "ekbf", [P, P], BF16)
    cume = alloc(S0, "cume", [P, P], F32)
    cumebf = alloc(S0, "cumebf", [P, P], BF16)
    onesq = alloc(S0, "onesq", [P, P], F32)
    stgpb = alloc(S0, "stgpb", [32, P], F32)
    bpst = alloc(S0, "bpst", [P, KB * P], BF16)
    junk = alloc(S0, "junk", [P, C], F32)
    wqst = alloc(S0, "wqst", [P, C], BF16)
    abs8 = alloc(S0, "abs8", [P, KB], F32)

    S1 = mk_region(RS0, RS_SZ)
    xbuf = [alloc(S1, f"xbuf{k}", [P, C], F32) for k in range(4)]
    r2buf = [alloc(S1, f"r2buf{k}", [P, C], BF16) for k in range(4)]

    S2 = mk_region(RS0, RS_SZ)
    xnre = [alloc(S2, f"xnre{k}", [P, C], F32) for k in range(4)]
    ln2b = [alloc(S2, f"ln2b{k}", [P, C], BF16) for k in range(4)]

    S3a = mk_region(RS0, RS_SZ)
    xqT = alloc(S3a, "xqT", [P, KB * NTOK], BF16)
    gst = [alloc(S3a, f"gst{k}", [P, NTOK], BF16) for k in range(2)]

    S3b = mk_region(RS0, RS_SZ)
    ln2T = alloc(S3b, "ln2T", [P, KB * NTOK], BF16)
    xatr = alloc(S3b, "xatr", [P, C], F32)
    xre = alloc(S3b, "xre", [P, C], F32)
    mlpT = alloc(S3b, "mlpT", [P, C], BF16)
    gT = alloc(S3b, "gT", [P, C], BF16)
    mst = [alloc(S3b, f"mst{k}", [P, NTOK], BF16) for k in range(2)]
    outb = alloc(S3b, "outb", [P, C], F32)

    C1 = mk_region(RC0, RC_SZ)
    smbig = alloc(C1, "smbig", [P, NT * 512], BF16)
    C2 = mk_region(RC0, RC_SZ)
    hank = [alloc(C2, f"hank{k}", [P, 32 * P], BF16) for k in range(2)]
    wtoep = [alloc(C2, f"wtoep{k}", [P, 32 * P], BF16) for k in range(2)]
    C3 = mk_region(RC0, RC_SZ)
    gbig = alloc(C3, "gbig", [P, HB * NTOK], BF16)
    C4 = mk_region(RC0, RC_SZ)
    wq_sb = alloc(C4, "wq_sb", [P, KB * C], BF16)
    r2T = [alloc(C4, f"r2T{k}", [P, KB * NTOK], BF16) for k in range(2)]

    A1 = mk_region(RA0, 65536)
    rawbig = alloc(A1, "rawbig", [P, NT * C], BF16)
    A2 = mk_region(RA0, 65536)
    fc1wT = alloc(A2, "fc1wT", [P, KB * HD], BF16)
    A3 = mk_region(RA0, 65536)
    ffst = alloc(A3, "ffst", [P, KB * C], F32)

    B1 = mk_region(RB0, 65536)
    yibig = alloc(B1, "yibig", [P, NT * C], BF16)
    B2 = mk_region(RB0, 65536)
    fc2wT = alloc(B2, "fc2wT", [P, HB * C], BF16)

    ctx = ExitStack()
    psA = ctx.enter_context(nc.psum_tensor("psA", [P, 512], F32))
    psB = ctx.enter_context(nc.psum_tensor("psB", [P, 512], F32))
    psc = [ctx.enter_context(nc.psum_tensor(f"psc{k}", [P, 1024], F32)) for k in range(2)]
    pst = [ctx.enter_context(nc.psum_tensor(f"pst{k}", [P, 512], F32)) for k in range(2)]

    class Sem:
        def __init__(self, name):
            self.h = nc.alloc_semaphore(name)
            self.v = 0
        def inc(self, instr, k=1):
            instr.then_inc(self.h, k)
            self.v += k
            return self.v
    SV = Sem("sv"); SS = Sem("ss"); SG = Sem("sg"); ST = Sem("stm"); CCS = Sem("ccs")
    qCAST = Sem("qcast"); qWSM = Sem("qwsm"); qP0 = Sem("qp0"); qPB = Sem("qpb")
    qFST = Sem("qfst"); qSH = Sem("qsh"); qKS = Sem("qks"); qBP = Sem("qbp")
    qWQ = Sem("qwq"); qMXT = Sem("qmxt"); qBPT = Sem("qbpt"); qWQL = Sem("qwql")
    qLD = [Sem(f"qld{k}") for k in range(4)]
    qSPX = [Sem(f"qspx{k}") for k in range(4)]
    qSPR = [Sem(f"qspr{k}") for k in range(4)]
    qLD2 = [Sem(f"qld2{k}") for k in range(4)]
    qSPA = [Sem(f"qspa{k}") for k in range(4)]
    qSPL = [Sem(f"qspl{k}") for k in range(4)]
    qHK = [Sem(f"qhk{k}") for k in range(2)]
    qW1T = Sem("qw1t"); qW2T = Sem("qw2t")
    qR2T = [Sem(f"qr2t{k}") for k in range(2)]
    qX2S = Sem("qx2s")
    qX2L = [Sem(f"qx2l{k}") for k in range(2)]
    qG = [Sem(f"qg{k}") for k in range(2)]
    qM = [Sem(f"qm{k}") for k in range(2)]
    qL2T = Sem("ql2t"); qMT = Sem("qmt"); qGT = Sem("qgt")
    qXA = Sem("qxa"); qXR = Sem("qxr"); qOUT = Sem("qout"); qDBG = Sem("qdbg")
    ALLQ = [qCAST, qWSM, qP0, qPB, qFST, qSH, qKS, qBP, qWQ, qMXT, qBPT, qWQL,
            qW1T, qW2T, qX2S, qL2T, qMT, qGT, qXA, qXR, qOUT, qDBG] + \
        qLD + qSPX + qSPR + qLD2 + qSPA + qSPL + qHK + qR2T + qX2L + qG + qM

    block = ctx.enter_context(nc.Block())
    sy, sc, ve, te, gp = nc.sync, nc.scalar, nc.vector, nc.tensor, nc.gpsimd

    def vfence():
        return SV.inc(ve.tensor_copy(vd[:], ceps5[:]))
    ve.memset(ceps5[:], LN0_EPS)
    ve.memset(s_raw[:], 0.0)
    ve.memset(smax[:], 0.0)
    ve.memset(geoj[:, 0:1], 0.0)
    sv_ceps = nc.alloc_semaphore("ceps")
    ve.memset(ceps3[:], LNF_EPS).then_inc(sv_ceps, 1)
    sc.wait_ge(sv_ceps, 1)

    for i in range(4):
        qCAST.inc(gp.dma_start(out=fc1w_bf[i * 1024:(i + 1) * 1024, :], in_=fc1w_in[i * 1024:(i + 1) * 1024, :]), 16)
    for i in range(4):
        qCAST.inc(gp.dma_start(out=fc2w_bf[i * 256:(i + 1) * 256, :], in_=fc2w_in[i * 256:(i + 1) * 256, :]), 16)
    qCAST.inc(gp.dma_start(out=mix_bf[:, 0:NST], in_=mix_in), 16)
    qCAST.inc(gp.dma_start(out=AP(mixsm_hbm.tensor, 0, [[P, 512], [1, NST]]), in_=mix_in[0:512, :]), 16)
    qCAST.inc(gp.dma_start(out=AP(mixsm_hbm.tensor, NST, [[P, 512], [1, NST]]), in_=mix_in[512:C, :]), 16)
    gq_casts = qCAST.v

    qP0.inc(sy.dma_start(out=ltau[0:NST, :], in_=log_tau), 16)
    qP0.inc(sy.dma_start(out=ltau[NST:P, :], in_=log_tau), 16)
    qP0.inc(sy.dma_start(out=r1[0:1, :], in_=log_dt), 16)
    qP0.inc(sy.dma_start(out=emask[:], in_=AP(emask_in.tensor, 0, [[0, P], [1, 1]])), 16)
    qP0.inc(sy.dma_start(out=omask[:], in_=AP(omask_in.tensor, 0, [[0, P], [1, 1]])), 16)
    qP0.inc(sy.dma_start(out=io1_bc[:], in_=AP(iota128.tensor, 0, [[0, P], [1, P]])), 16)
    qP0.inc(sy.dma_start(out=io2_bc[:], in_=AP(iota128p1.tensor, 0, [[0, P], [1, P]])), 16)
    qP0.inc(sy.dma_start(out=j128_bc[:], in_=AP(j128_in.tensor, 0, [[0, P], [1, NT + 1]])), 16)
    qP0.inc(sy.dma_start(out=rev127[:], in_=rev127_in), 16)
    qP0.inc(sy.dma_start(out=pcol1[:], in_=AP(iota128p1.tensor, 0, [[1, P], [1, 1]])), 16)
    pb_jobs = [(w1_pb, n1w_in, KB), (b1_pb, n1b_in, KB), (n2w_pb, n2w_in, KB),
               (n2b_pb, n2b_in, KB), (d_pb, d_in, KB), (g_pb, lng_in, KB),
               (bta_pb, lnb_in, KB), (fc2b_pb, fc2b_in, KB), (fc1b_pb, fc1b_in, HB)]
    sv_pb = SV.v
    for dst, src, nb in pb_jobs:

        sq_st = qPB.inc(sy.dma_start(out=stgpb[0:nb, :], in_=AP(src.tensor, 0, [[P, nb], [1, P]])), 16)
        ve.wait_ge(qPB.h, sq_st)
        for jb in range(4):
            sv_pb = SV.inc(ve.transpose(dst[32 * jb:32 * (jb + 1), 0:32], stgpb[0:32, 32 * jb:32 * (jb + 1)]))
        sy.wait_ge(SV.h, sv_pb)
    qWSM.inc(gp.dma_start(out=w1sm[0:NST, :], in_=AP(n1w_in.tensor, 0, [[0, NST], [1, 512]])), 16)
    qWSM.inc(gp.dma_start(out=w1sm[NST:P, :], in_=AP(n1w_in.tensor, 512, [[0, NST], [1, 512]])), 16)
    qWSM.inc(gp.dma_start(out=b1sm[0:NST, :], in_=AP(n1b_in.tensor, 0, [[0, NST], [1, 512]])), 16)
    gq_sm = qWSM.inc(gp.dma_start(out=b1sm[NST:P, :], in_=AP(n1b_in.tensor, 512, [[0, NST], [1, 512]])), 16)
    for i in range(KB):
        qFST.inc(sy.dma_start(out=ffst[:, i * C:(i + 1) * C], in_=ffnw_in[i * P:(i + 1) * P, :]), 16)
    sq_p0 = qP0.v; sq_fst = qFST.v

    sc.wait_ge(qP0.h, sq_p0)
    ve.wait_ge(qP0.h, sq_p0)
    SV.inc(ve.tensor_scalar(ident[:], io2_bc[:], pcol1[:, 0:1], None, ALU.is_equal))
    sc.activation(r2[:], ltau[:], AF.Abs)
    sc.activation(r3[:], r2[:], AF.Exp, scale=-1.0)
    sc.activation(r4[:], r3[:], AF.Ln, bias=1.0)
    ss_sp1 = SS.inc(sc.activation(r5[:], ltau[:], AF.Relu))
    ve.wait_ge(SS.h, ss_sp1)
    sv_tau = SV.inc(ve.tensor_add(tauv[:], r4[:], r5[:]))
    sc.wait_ge(SV.h, sv_tau)
    sc.activation(r2[0:1, :], r1[0:1, :], AF.Abs)
    sc.activation(r3[0:1, :], r2[0:1, :], AF.Exp, scale=-1.0)
    sc.activation(r4[0:1, :], r3[0:1, :], AF.Ln, bias=1.0)
    ss_sp2 = SS.inc(sc.activation(r5[0:1, :], r1[0:1, :], AF.Relu))
    ve.wait_ge(SS.h, ss_sp2)
    SV.inc(ve.tensor_add(r1[0:1, :], r4[0:1, :], r5[0:1, :]))
    sv_dt = vfence()
    sy.wait_ge(SV.h, sv_dt)
    sq_dtw = qSH.inc(sy.dma_start(out=scal_hbm[:, 0:1], in_=r1[0:1, :]), 16)
    sy.wait_ge(qSH.h, sq_dtw)
    sq_dtr = qSH.inc(sy.dma_start(out=dtbc[:], in_=AP(scal_hbm.tensor, 0, [[0, P], [1, 1]])), 16)
    ve.wait_ge(qSH.h, sq_dtr)
    ve.wait_ge(SV.h, sv_tau)
    SV.inc(ve.tensor_scalar(negp[:], tauv[:], dtbc[:], -1.0, ALU.mult, ALU.mult))
    sv_negp = vfence()
    sy.wait_ge(SV.h, sv_negp)
    sq_pw = qSH.inc(sy.dma_start(out=pbc_hbm[:, :], in_=negp[0:NST, :]), 16)
    sy.wait_ge(qSH.h, sq_pw)
    sq_pr = qSH.inc(sy.dma_start(out=prow_bc[:], in_=AP(pbc_hbm.tensor, 0, [[0, P], [1, NST]])), 16)

    sc.wait_ge(qSH.h, sq_pr)
    sc.activation(einT[:], prow_bc[:], AF.Exp, scale=rev127[:])
    sc.activation(aoutD[:], io2_bc[:], AF.Exp, scale=negp[:])
    sc.activation(ek32[:], io1_bc[:], AF.Exp, scale=negp[:])
    sc.activation(ekbf[:], io1_bc[:], AF.Exp, scale=negp[:])
    sc.activation(dec[:], negp[:], AF.Exp, scale=128.0)
    ss_tabs = SS.inc(sc.activation(npj[:], j128_bc[:], AF.Identity, scale=negp[:]))

    ve.wait_ge(SS.h, ss_tabs)
    ve.memset(onesq[:], 1.0)
    sv_scan = SV.inc(ve.tensor_tensor_scan(cume[:], onesq[:], ek32[:], 0.0, ALU.mult, ALU.add))
    ve.wait_ge(SV.h, sv_scan)
    sv_cume = SV.inc(ve.tensor_copy(cumebf[:], cume[:]))
    SV.inc(ve.reduce_sum(geoL[:], ek32[:], axis=mybir.AxisListType.X))
    sv_geo0 = SV.v
    sc.wait_ge(SV.h, sv_geo0)
    for j in range(NT):
        ssj = SS.inc(sc.activation(r2[:], npj[:, j:j + 1], AF.Exp))
        ve.wait_ge(SS.h, ssj)
        svj = SV.inc(ve.scalar_tensor_tensor(geoj[:, j + 1:j + 2], r2[:], geoL[:], geoj[:, j:j + 1], ALU.mult, ALU.add))
        if j < NT - 1:
            sc.wait_ge(SV.h, svj)
    sv_geoj = SV.v

    sc.wait_ge(qCAST.h, gq_casts)
    qMXT.inc(sc.dma_start(out=mixT[:], in_=mix_bf, transpose=True), 16)
    aq_mixT = qMXT.inc(sc.dma_start(out=mixT_sm[:], in_=mixsm_hbm, transpose=True), 16)

    ve.wait_ge(qMXT.h, aq_mixT)
    ve.wait_ge(qWSM.h, gq_sm)
    ve.tensor_mul(mixpT_sm[:], mixT_sm[:], w1sm[:])
    ve.tensor_mul(bm1[:], mixT_sm[:], b1sm[:])
    ve.tensor_scalar(b1g[:], b1sm[:], geoj[:, NT:NT + 1], None, ALU.mult)
    sv_mixp = SV.inc(ve.tensor_mul(dw1_pb[:], d_pb[:, 0:KB], w1_pb[:, 0:KB]))

    te.wait_ge(qMXT.h, aq_mixT)
    te.wait_ge(SV.h, sv_cume)
    sv_evac = SV.v
    ve.wait_ge(SV.h, sv_mixp)
    ve.memset(ks_st[:], 0.0)
    sq_ks = []
    for cb in range(KB):
        if cb > 0:
            te.wait_ge(SV.h, sv_evac)
        te.matmul(pst[0][:, 0:P], mixT[0:NST, cb * P:(cb + 1) * P], ekbf[0:NST, :], start=True, stop=True)
        st_kl = ST.inc(te.matmul(pst[1][:, 0:P], mixT[0:NST, cb * P:(cb + 1) * P], cumebf[0:NST, :], start=True, stop=True))
        ve.wait_ge(ST.h, st_kl)
        if cb > 0:
            ve.wait_ge(qKS.h, sq_ks[-1])
        rev_dst = AP(ks_st, 127, [[256, P], [-1, P]])
        ve.tensor_scalar(rev_dst, pst[0][:, 0:P], w1_pb[:, cb:cb + 1], None, ALU.mult)
        ve.tensor_scalar(ks_st[:, 127:128], ks_st[:, 127:128], dw1_pb[:, cb:cb + 1], None, ALU.add)
        SV.inc(ve.tensor_scalar(bpst[:, cb * P:(cb + 1) * P], pst[1][:, 0:P], d_pb[:, cb:cb + 1], b1_pb[:, cb:cb + 1], ALU.add, ALU.mult))
        sv_evac = vfence()
        sy.wait_ge(SV.h, sv_evac)
        sq_ks.append(qKS.inc(sy.dma_start(out=ksrc_hbm[cb * P:(cb + 1) * P, :], in_=ks_st[:]), 16))
    sq_bpw = qBP.inc(sy.dma_start(
        out=AP(bpre_cm.tensor, 0, [[P, P], [P * P, KB], [1, P]]),
        in_=AP(bpst, 0, [[KB * P, P], [P, KB], [1, P]])), 16)
    sc.wait_ge(qBP.h, sq_bpw)
    aq_bpre = qBPT.inc(sc.dma_start(out=bpre[:], in_=bpre_cm, transpose=True), 16)

    sc.wait_ge(qFST.h, sq_fst)
    for cb in range(KB):
        ssab = SS.inc(sc.activation(junk[:], ffst[:, cb * C:(cb + 1) * C], AF.Abs, accum_out=abs8[:, cb:cb + 1]))
    ss_abs = SS.v
    ve.wait_ge(SS.h, ss_abs)
    SV.inc(ve.reduce_sum(r2[:], abs8[:], axis=mybir.AxisListType.X))
    sv_abs = vfence()
    sy.wait_ge(SV.h, sv_abs)
    sq_abw = qSH.inc(sy.dma_start(out=cc_gin[:, :], in_=r2[:]), 16)
    sy.wait_ge(qSH.h, sq_abw)
    sq_abr = qSH.inc(sy.dma_start(out=srow[:], in_=cc_gin), 16)
    ve.wait_ge(qSH.h, sq_abr)
    svt = SV.inc(ve.reduce_sum(sc1[:, 0:1], srow[:], axis=mybir.AxisListType.X))
    ve.wait_ge(SV.h, svt)
    svt = SV.inc(ve.tensor_scalar(sc1[:, 7:8], sc1[:, 0:1], 1.0 / (C * C), None, ALU.mult))
    ve.wait_ge(SV.h, svt)
    svt = SV.inc(ve.tensor_scalar(sc1[:, 1:2], sc1[:, 7:8], 1.0, FFN_EPS, ALU.mult, ALU.add))
    ve.wait_ge(SV.h, svt)
    SV.inc(ve.reciprocal(sc1[:, 2:3], sc1[:, 1:2]))
    sv_beta = vfence()
    sy.wait_ge(SV.h, sv_beta)
    sq_bw = qSH.inc(sy.dma_start(out=scal_hbm[:, 1:3], in_=sc1[:, 1:3]), 16)
    sy.wait_ge(qSH.h, sq_bw)
    sq_br = qSH.inc(sy.dma_start(out=rmx[:], in_=AP(scal_hbm.tensor, 2, [[0, P], [1, 1]])), 16)
    ve.wait_ge(qSH.h, sq_br)
    sq_wq = qWQ.v
    for cb in range(KB):
        if cb > 0:
            ve.wait_ge(qWQ.h, sq_wq)
        ve.tensor_scalar(junk[:], ffst[:, cb * C:(cb + 1) * C], rmx[:], MAGIC, ALU.mult, ALU.add)
        ve.tensor_scalar(junk[:], junk[:], MAGIC, 1.0, ALU.subtract, ALU.min)
        SV.inc(ve.tensor_scalar(wqst[:], junk[:], -1.0, None, ALU.max))
        sv_w = vfence()
        sy.wait_ge(SV.h, sv_w)
        sq_wq = qWQ.inc(sy.dma_start(out=wq_hbm[:, cb * C:(cb + 1) * C], in_=wqst[:]), 16)
    sv_p0_v = SV.v; ss_p0_s = SS.v; st_p0 = ST.v
    if "ksrc" in dbg_o:
        sy.wait_ge(qKS.h, sq_ks[-1])
        qDBG.inc(sy.dma_start(out=dbg_o["ksrc"], in_=ksrc_hbm), 16)
    if "wq" in dbg_o:
        sy.wait_ge(qWQ.h, sq_wq)
        qDBG.inc(sy.dma_start(out=dbg_o["wq"], in_=wq_hbm), 16)
    if "bpre" in dbg_o:
        sy.wait_ge(qBPT.h, aq_bpre)
        qDBG.inc(sy.dma_start(out=dbg_o["bpre"], in_=bpre[:]), 16)

    sy.wait_ge(SV.h, sv_p0_v)
    sy.wait_ge(SS.h, ss_p0_s)
    sy.wait_ge(qWQ.h, sq_wq)
    sy.wait_ge(qBPT.h, aq_bpre)
    sy.wait_ge(qKS.h, qKS.v)
    sy.wait_ge(qBP.h, qBP.v)
    sy.wait_ge(qPB.h, qPB.v)
    sy.wait_ge(ST.h, ST.v)
    sq_x = {}; ss_xn = {}; sv_r2 = {}; sq_xn = {}; sq_r2 = {}
    GW = 4

    def rsig_group(mvg, eps_ap, rsgg, nmg):
        var4 = AP(mvg, 1, [[8, P], [2, GW]])
        mu4 = AP(mvg, 0, [[8, P], [2, GW]])
        ss0 = SS.inc(sc.activation(rt4[:, 0:GW], var4, AF.Ln, bias=eps_ap))
        sc.wait_ge(SS.h, ss0)
        ss1 = SS.inc(sc.activation(rsgg[:, 0:GW], rt4[:, 0:GW], AF.Exp, scale=-0.5))
        ve.wait_ge(SS.h, ss1)
        return SV.inc(ve.scalar_tensor_tensor(nmg[:, 0:GW], mu4, -1.0, rsgg[:, 0:GW], ALU.mult, ALU.mult))

    for g in range(NT // GW):
        tiles = range(g * GW, (g + 1) * GW)
        for i in tiles:
            sl = i % GW
            if i >= GW:
                sy.wait_ge(qSPX[sl].h, qSPX[sl].v)
                sy.wait_ge(SV.h, sv_r2[i - GW])
            sq_x[i] = qLD[sl].inc(sy.dma_start(out=xbuf[sl][:], in_=x_in[i * P:(i + 1) * P, :]), 16)
        for i in tiles:
            sl = i % GW
            ve.wait_ge(qLD[sl].h, sq_x[i])
            SV.inc(ve.bn_stats(stA_g[:, sl * 12:sl * 12 + 6], xbuf[sl][:, 0:512]))
            svb = SV.inc(ve.bn_stats(stA_g[:, sl * 12 + 6:sl * 12 + 12], xbuf[sl][:, 512:1024]))
            ve.wait_ge(SV.h, svb)
            sq_x[(i, "a0")] = SV.inc(ve.bn_aggr(mv0g[:, 2 * sl:2 * sl + 2], stA_g[:, sl * 12:sl * 12 + 12]))
        sc.wait_ge(SV.h, sq_x[(tiles[-1], "a0")])
        sv_nm0 = rsig_group(mv0g, ceps5[:], rsg0g, nm0g)
        sc.wait_ge(SV.h, sv_nm0)
        for i in tiles:
            sl = i % GW
            ss_xn[i] = SS.inc(sc.activation(xbuf[sl][:], xbuf[sl][:], AF.Identity,
                                            scale=rsg0g[:, sl:sl + 1], bias=nm0g[:, sl:sl + 1]))
        for i in tiles:
            sl = i % GW
            ve.wait_ge(SS.h, ss_xn[i])
            SV.inc(ve.bn_stats(stA_g[:, sl * 12:sl * 12 + 6], xbuf[sl][:, 0:512]))
            svb = SV.inc(ve.bn_stats(stA_g[:, sl * 12 + 6:sl * 12 + 12], xbuf[sl][:, 512:1024]))
            ve.wait_ge(SV.h, svb)
            sq_x[(i, "a1")] = SV.inc(ve.bn_aggr(mv1g[:, 2 * sl:2 * sl + 2], stA_g[:, sl * 12:sl * 12 + 12]))
        sc.wait_ge(SV.h, sq_x[(tiles[-1], "a1")])
        rsig_group(mv1g, ceps5[:], rsg1g, nm1g)
        sv_nm2 = rsig_group(mv1g, ceps3[:], rsg2g, nm2g)
        ve.wait_ge(SV.h, sv_nm2)
        for i in tiles:
            sl = i % GW
            if i >= GW:
                ve.wait_ge(qSPR[sl].h, qSPR[sl].v)
            ve.tensor_scalar(rawbig[:, i * C:(i + 1) * C], xbuf[sl][:], rsg1g[:, sl:sl + 1], nm1g[:, sl:sl + 1], ALU.mult, ALU.add)
            sv_r2[i] = SV.inc(ve.tensor_scalar(r2buf[sl][:], xbuf[sl][:], rsg2g[:, sl:sl + 1], nm2g[:, sl:sl + 1], ALU.mult, ALU.add))
        svf = vfence()
        for i in tiles:
            sv_r2[i] = svf
        for i in tiles:
            sl = i % GW
            sy.wait_ge(SV.h, sv_r2[i])
            sq_xn[i] = qSPX[sl].inc(sy.dma_start(out=xn_sp[i * P:(i + 1) * P, :], in_=xbuf[sl][:]), 16)
            sq_r2[i] = qSPR[sl].inc(sy.dma_start(out=raw2_sp[i * P:(i + 1) * P, :], in_=r2buf[sl][:]), 16)

    if "xn" in dbg_o:
        for k in range(4):
            sy.wait_ge(qSPX[k].h, qSPX[k].v)
        qDBG.inc(sy.dma_start(out=dbg_o["xn"], in_=xn_sp), 16)
    if "raw2" in dbg_o:
        for k in range(4):
            sy.wait_ge(qSPR[k].h, qSPR[k].v)
        qDBG.inc(sy.dma_start(out=dbg_o["raw2"], in_=raw2_sp), 16)

    st_agg = {}; sv_s = {}
    for j in range(NT):
        te.wait_ge(SV.h, sv_r2[j])
        if j >= 2:
            te.wait_ge(SV.h, sv_s[j - 2])
        pa = psA if j % 2 == 0 else psB
        rb = rawbig[:, j * C:(j + 1) * C]
        te.matmul(pa[0:NST, 0:512], einT[:], rb[:, 0:512], start=True, stop=True)
        st_agg[j] = ST.inc(te.matmul(pa[NST:P, 0:512], einT[:], rb[:, 512:1024], start=True, stop=True, tile_position=(0, NST)))
        ve.wait_ge(ST.h, st_agg[j])
        ve.tensor_mul(smbig[:, j * 512:(j + 1) * 512], s_raw[:], mixpT_sm[:])
        sv_s[j] = SV.inc(ve.scalar_tensor_tensor(s_raw[:], s_raw[:], dec[:], pa[:, 0:512], ALU.mult, ALU.add))
    if "sm" in dbg_o:
        sy.wait_ge(SV.h, sv_s[NT - 1])
        qDBG.inc(sy.dma_start(out=dbg_o["sm"], in_=smbig[:]), 16)

    ve.tensor_mul(s_tmp[:], s_raw[:], w1sm[:])
    ve.tensor_add(s_tmp[:], s_tmp[:], b1g[:])
    SV.inc(ve.tensor_scalar(s_tmp[:], s_tmp[:], emask[:], None, ALU.mult))
    sv_send = vfence()
    sy.wait_ge(SV.h, sv_send)
    sq_ccw = qSH.inc(sy.dma_start(out=cc_sin, in_=s_tmp[:]), 16)
    gp.wait_ge(qSH.h, sq_ccw)
    cc1 = CCS.inc(gp.collective_compute(
        "AllReduce", ALU.add, replica_groups=[[0, 1], [2, 3], [4, 5], [6, 7]],
        ins=[cc_sin], outs=[cc_sout]))
    sy.wait_ge(CCS.h, cc1)
    sq_ccr = qSH.inc(sy.dma_start(out=s_tmp[:], in_=cc_sout), 16)
    ve.wait_ge(qSH.h, sq_ccr)
    ve.tensor_scalar(s_tmp[:], s_tmp[:], omask[:], None, ALU.mult)
    SV.inc(ve.tensor_mul(zqb[:], s_tmp[:], mixT_sm[:]))
    sv_zqb = vfence()

    te.wait_ge(SV.h, sv_zqb)
    te.wait_ge(qBPT.h, aq_bpre)
    sv_yi = {}
    ss_aj = {}; sv_agj = {}
    for j in range(NT):
        pb2 = j % 2
        sc.wait_ge(SV.h, sv_zqb if j < 2 else sv_yi[j - 2])
        ss_aj[j] = SS.inc(sc.activation(ajb[pb2][:], io2_bc[:], AF.Exp, scale=negp[:], bias=npj[:, j:j + 1]))
        ve.wait_ge(SS.h, ss_aj[j])
        SV.inc(ve.tensor_scalar(agjb[pb2][:], aoutD[:], geoj[:, j:j + 1], None, ALU.mult))
        sv_agj[j] = vfence()
        te.wait_ge(SV.h, sv_agj[j])
        if j >= 2:
            te.wait_ge(SV.h, sv_yi[j - 2])
        pc = psc[pb2]
        smj = smbig[:, j * 512:(j + 1) * 512]
        te.matmul(pc[:, 0:512], aoutD[0:NST, :], smj[0:NST, :], start=True, stop=False)
        te.matmul(pc[:, 0:512], agjb[pb2][0:NST, :], bm1[0:NST, :], start=False, stop=False)
        te.matmul(pc[:, 0:512], ajb[pb2][0:NST, :], zqb[0:NST, :], start=False, stop=True)
        te.matmul(pc[:, 512:1024], aoutD[NST:P, :], smj[NST:P, :], start=True, stop=False, tile_position=(NST, 0))
        te.matmul(pc[:, 512:1024], agjb[pb2][NST:P, :], bm1[NST:P, :], start=False, stop=False, tile_position=(NST, 0))
        st_c = ST.inc(te.matmul(pc[:, 512:1024], ajb[pb2][NST:P, :], zqb[NST:P, :], start=False, stop=True, tile_position=(NST, 0)))
        ve.wait_ge(ST.h, st_c)
        yi_dst = AP(yibig, j, [[NT * C, P], [32, C]])
        sv_yi[j] = SV.inc(ve.tensor_tensor(yi_dst, pc[:], bpre[:], ALU.add))
    sv_yi_all = SV.v

    NGRP = 32
    gq_h = {}; sv_rev = {}; st_tp = {}; sv_tev = {}
    gp.wait_ge(qKS.h, sq_ks[-1])
    gp.wait_ge(ST.h, ST.v)
    gp.wait_ge(SV.h, sv_yi_all)
    if "sm" in dbg_o:
        gp.wait_ge(qDBG.h, qDBG.v)
    for g in range(NGRP):
        hb2 = g % 2
        if g >= 2:
            gp.wait_ge(SV.h, sv_rev[g - 2])
        src = AP(ksrc_hbm.tensor, g * 32 * 256, [[1, P], [256, 32], [1, P]])
        gq_h[g] = qHK[hb2].inc(gp.dma_start(out=hank[hb2][:].rearrange("p (c j) -> p c j", c=32), in_=src), 16)
        ve.wait_ge(qHK[hb2].h, gq_h[g])
        if g >= 2:
            ve.wait_ge(ST.h, st_tp[g - 2])
        rev_src = AP(hank[hb2], 127, [[32 * P, P], [P, 32], [-1, P]])
        SV.inc(ve.tensor_copy(wtoep[hb2][:].rearrange("p (c j) -> p c j", c=32), rev_src))
        sv_rev[g] = vfence()
        te.wait_ge(SV.h, sv_rev[g])
        if g >= 2:
            te.wait_ge(SV.h, sv_tev[g - 2])
        tp_pair = (pst[0], pst[1]) if g % 2 == 0 else (psA, psB)
        mm = None
        for cl in range(32):
            pt = tp_pair[cl // 16]
            mov = AP(rawbig, g * 32 + cl, [[NT * C, P], [C, NT]])
            mm = te.matmul(pt[:, (cl % 16) * 32:(cl % 16) * 32 + 32], wtoep[hb2][:, cl * P:(cl + 1) * P], mov, start=True, stop=True)
        st_tp[g] = ST.inc(mm)
        ve.wait_ge(ST.h, st_tp[g])
        ve.tensor_tensor(yibig[:, g * 1024:g * 1024 + 512], yibig[:, g * 1024:g * 1024 + 512], tp_pair[0][:, 0:512], ALU.add)
        SV.inc(ve.tensor_tensor(yibig[:, g * 1024 + 512:(g + 1) * 1024], yibig[:, g * 1024 + 512:(g + 1) * 1024], tp_pair[1][:, 0:512], ALU.add))
        sv_tev[g] = vfence()
    sv_tev_all = SV.v
    if "yi" in dbg_o:
        sy.wait_ge(SV.h, sv_tev_all)
        qDBG.inc(sy.dma_start(out=dbg_o["yi"], in_=yibig[:]), 16)
    if "raw1" in dbg_o:
        sy.wait_ge(ST.h, st_tp[NGRP - 1])
        qDBG.inc(sy.dma_start(out=AP(dbg_o["raw1"].tensor, 0, [[C, P], [C * P, NT], [1, C]]),
                            in_=AP(rawbig, 0, [[NT * C, P], [C, NT], [1, C]])), 16)

    sy.wait_ge(SV.h, sv_r2[NT - 1])
    for k in range(4):
        sy.wait_ge(qSPX[k].h, qSPX[k].v)
    sq_xnr = {}; sg_xat = {}; ss_ln2 = {}; sq_xat = {}; sq_ln2 = {}
    sv_a3 = {}
    sc.wait_ge(ST.h, st_tp[NGRP - 1])
    sc.wait_ge(qCAST.h, gq_casts)
    aq_w1t = {}
    for kb in range(KB):
        aq_w1t[kb] = qW1T.inc(sc.dma_start(out=fc1wT[:, kb * HD:(kb + 1) * HD], in_=fc1w_bf[:, kb * P:(kb + 1) * P], transpose=True), 16)
    for g in range(NT // GW):
        tiles = range(g * GW, (g + 1) * GW)
        for i in tiles:
            sl = i % GW
            if i >= GW:
                sy.wait_ge(qSPA[sl].h, qSPA[sl].v)
                sy.wait_ge(SS.h, ss_ln2[i - GW])
            sq_xnr[i] = qLD2[sl].inc(sy.dma_start(out=xnre[sl][:], in_=xn_sp[i * P:(i + 1) * P, :]), 16)
        for i in tiles:
            sl = i % GW
            gp.wait_ge(qLD2[sl].h, sq_xnr[i])
            if i == 0:
                gp.wait_ge(SV.h, sv_tev_all)
            yi_view = AP(yibig, i, [[NT * C, P], [32, C]])
            sg_xat[i] = SG.inc(gp.tensor_tensor(xnre[sl][:], xnre[sl][:], yi_view, ALU.add))
        for i in tiles:
            sl = i % GW
            ve.wait_ge(SG.h, sg_xat[i])
            SV.inc(ve.bn_stats(stA_g[:, sl * 12:sl * 12 + 6], xnre[sl][:, 0:512]))
            svb = SV.inc(ve.bn_stats(stA_g[:, sl * 12 + 6:sl * 12 + 12], xnre[sl][:, 512:1024]))
            ve.wait_ge(SV.h, svb)
            sv_a3[i] = SV.inc(ve.bn_aggr(mv0g[:, 2 * sl:2 * sl + 2], stA_g[:, sl * 12:sl * 12 + 12]))
        sc.wait_ge(SV.h, sv_a3[tiles[-1]])
        sv_nm3 = rsig_group(mv0g, ceps5[:], rsg3g, nm3g)
        sc.wait_ge(SV.h, sv_nm3)
        for i in tiles:
            sl = i % GW
            if i >= GW:
                sc.wait_ge(qSPL[sl].h, qSPL[sl].v)
            ss_ln2[i] = SS.inc(sc.activation(ln2b[sl][:], xnre[sl][:], AF.Identity,
                                             scale=rsg3g[:, sl:sl + 1], bias=nm3g[:, sl:sl + 1]))
        for i in tiles:
            sl = i % GW
            sy.wait_ge(SS.h, ss_ln2[i])
            sq_xat[i] = qSPA[sl].inc(sy.dma_start(out=xattn_sp[i * P:(i + 1) * P, :], in_=xnre[sl][:]), 16)
            sq_ln2[i] = qSPL[sl].inc(sy.dma_start(out=ln2_sp[i * P:(i + 1) * P, :], in_=ln2b[sl][:]), 16)
    if "xattn" in dbg_o:
        for k in range(4):
            sy.wait_ge(qSPA[k].h, qSPA[k].v)
        qDBG.inc(sy.dma_start(out=dbg_o["xattn"], in_=xattn_sp), 16)
    if "ln2" in dbg_o:
        for k in range(4):
            sy.wait_ge(qSPL[k].h, qSPL[k].v)
        qDBG.inc(sy.dma_start(out=dbg_o["ln2"], in_=ln2_sp), 16)

    sc.wait_ge(SG.h, sg_xat[NT - 1])
    aq_w2t = {}
    for hb in range(HB):
        aq_w2t[hb] = qW2T.inc(sc.dma_start(out=fc2wT[:, hb * C:(hb + 1) * C], in_=fc2w_bf[:, hb * P:(hb + 1) * P], transpose=True), 16)
    sv_n2bf = SV.inc(ve.tensor_copy(n2b_bf[:], n2b_pb[:, 0:KB]))
    te.wait_ge(SV.h, sv_n2bf)
    te.wait_ge(SV.h, sv_yi_all)
    te.wait_ge(qW1T.h, aq_w1t[KB - 1])
    for hb in range(HB):
        for kb in range(KB):
            ins_mm = te.matmul(psc[0][:, hb:hb + 1], fc1wT[:, kb * HD + hb * P:kb * HD + (hb + 1) * P],
                               n2b_bf[:, kb:kb + 1], start=(kb == 0), stop=(kb == KB - 1))
    st_bf = ST.inc(ins_mm)
    ve.wait_ge(ST.h, st_bf)
    SV.inc(ve.tensor_add(b1p[:, 0:HB], psc[0][:, 0:HB], fc1b_pb[:, 0:HB]))
    sv_b1p = vfence()
    sv_w1s = {}
    for kb in range(KB):
        sv_w1s[kb] = SV.inc(ve.tensor_scalar(fc1wT[:, kb * HD:(kb + 1) * HD], fc1wT[:, kb * HD:(kb + 1) * HD], n2w_pb[:, kb:kb + 1], None, ALU.mult))
    sv_wscale = vfence()

    for k in range(4):
        sc.wait_ge(qSPL[k].h, qSPL[k].v)
        sc.wait_ge(qSPR[k].h, qSPR[k].v)
    sc.wait_ge(SG.h, sg_xat[NT - 1])
    aq_r2t = {}; ss_xn2 = {}; sq_xn2 = {}; sv_mx = {}
    for tt in range(T3):
        b = tt % 2
        for kb in range(KB):
            if tt >= 2:
                sc.wait_ge(SS.h, ss_xn2[tt - 2])
            aq_r2t[(tt, kb)] = qR2T[b].inc(sc.dma_start(
                out=r2T[b][:, kb * NTOK:(kb + 1) * NTOK],
                in_=raw2_sp[tt * NTOK:(tt + 1) * NTOK, kb * P:(kb + 1) * P], transpose=True), 16)
        sc.wait_ge(qR2T[b].h, aq_r2t[(tt, KB - 1)])
        if tt >= 1:
            sc.wait_ge(qX2S.h, sq_xn2[tt - 1])
            sc.wait_ge(SV.h, sv_mx[tt - 1])
        for kb in range(KB):
            ins_aff = sc.activation(xqT[:, kb * NTOK:(kb + 1) * NTOK], r2T[b][:, kb * NTOK:(kb + 1) * NTOK],
                                    AF.Identity, scale=g_pb[:, kb:kb + 1], bias=bta_pb[:, kb:kb + 1])
        ss_xn2[tt] = SS.inc(ins_aff)
        ve.wait_ge(SS.h, ss_xn2[tt])
        svt = SV.inc(ve.reduce_max(rmx[:], xqT[:], axis=mybir.AxisListType.X, apply_absolute_value=True))
        ve.wait_ge(SV.h, svt)
        SV.inc(ve.tensor_max(smax[:], smax[:], rmx[:]))
        sv_mx[tt] = vfence()
        sy.wait_ge(SS.h, ss_xn2[tt])
        sq_xn2[tt] = qX2S.inc(sy.dma_start(
            out=AP(xn2_sp.tensor, tt * NTOK, [[TLOC, P], [P * TLOC, KB], [1, NTOK]]),
            in_=AP(xqT, 0, [[KB * NTOK, P], [NTOK, KB], [1, NTOK]])), 16)
    sy.wait_ge(SV.h, sv_mx[T3 - 1])
    sq_gw = qSH.inc(sy.dma_start(out=cc_gin[:, :], in_=smax[:]), 16)
    gp.wait_ge(qSH.h, sq_gw)
    cc2 = CCS.inc(gp.collective_compute(
        "AllReduce", ALU.max, replica_groups=[list(range(8))],
        ins=[cc_gin], outs=[cc_gout]))
    sy.wait_ge(CCS.h, cc2)
    sq_gr = qSH.inc(sy.dma_start(out=srow[:], in_=cc_gout), 16)
    ve.wait_ge(qSH.h, sq_gr)
    svt = SV.inc(ve.reduce_max(sc1[:, 3:4], srow[:], axis=mybir.AxisListType.X))
    ve.wait_ge(SV.h, svt)
    svt = SV.inc(ve.reciprocal(sc1[:, 4:5], sc1[:, 3:4]))
    ve.wait_ge(SV.h, svt)
    svt = SV.inc(ve.tensor_scalar(sc1[:, 5:6], sc1[:, 4:5], 128.0, None, ALU.mult))
    ve.wait_ge(SV.h, svt)
    svt = SV.inc(ve.tensor_tensor(sc1[:, 6:7], sc1[:, 3:4], sc1[:, 7:8], ALU.mult))
    ve.wait_ge(SV.h, svt)
    SV.inc(ve.tensor_scalar(sc1[:, 6:7], sc1[:, 6:7], 1.0 / 128.0, None, ALU.mult))
    sv_q = vfence()
    sy.wait_ge(SV.h, sv_q)
    sq_qw = qSH.inc(sy.dma_start(out=scal_hbm[:, 4:7], in_=sc1[:, 4:7]), 16)
    sy.wait_ge(qSH.h, sq_qw)
    qSH.inc(sy.dma_start(out=rqbc[:], in_=AP(scal_hbm.tensor, 5, [[0, P], [1, 1]])), 16)
    sq_ysc = qSH.inc(sy.dma_start(out=ysc[:], in_=AP(scal_hbm.tensor, 6, [[0, P], [1, 1]])), 16)
    if "scal" in dbg_o:
        sq_sc16 = qSH.inc(sy.dma_start(out=scal_hbm[:, 16:32], in_=sc1[:, 0:16]), 16)
        sy.wait_ge(qSH.h, sq_sc16)
        qDBG.inc(sy.dma_start(out=dbg_o["scal"], in_=scal_hbm), 16)

    sy.wait_ge(qWQ.h, sq_wq)
    sq_wql = qWQL.inc(sy.dma_start(out=wq_sb[:], in_=wq_hbm), 16)
    st_ffn = {}; ss_g = {}; sq_g = {}
    sq_x2l = {}
    sy.wait_ge(SS.h, ss_xn2[T3 - 1])
    sy.wait_ge(qX2S.h, qX2S.v)
    for tt in range(T3):
        b = tt % 2
        if tt >= 2:
            sy.wait_ge(SV.h, sv_xq[tt - 2])
        sq_x2l[tt] = qX2L[b].inc(sy.dma_start(
            out=AP(r2T[b], 0, [[KB * NTOK, P], [NTOK, KB], [1, NTOK]]),
            in_=AP(xn2_sp.tensor, tt * NTOK, [[TLOC, P], [P * TLOC, KB], [1, NTOK]])), 16)
        ve.wait_ge(qX2L[b].h, sq_x2l[tt])
        ve.wait_ge(qSH.h, sq_ysc)
        if tt >= 1:
            ve.wait_ge(ST.h, st_ffn[(tt - 1, KB - 1)])
        ve.tensor_scalar(xqT[:], r2T[b][:], rqbc[:], 128.0 - FFN_EPS, ALU.mult, ALU.min)
        if tt == 0:
            sv_xq = {}
        SV.inc(ve.tensor_scalar(xqT[:], xqT[:], -(128.0 - FFN_EPS), None, ALU.max))
        sv_xq[tt] = vfence()
        te.wait_ge(SV.h, sv_xq[tt])
        te.wait_ge(qWQL.h, sq_wql)
        for ub in range(KB):
            pf = pst[ub % 2]
            prev = (tt, ub - 2) if ub >= 2 else ((tt - 1, KB - 2 + ub) if tt > 0 else None)
            if prev is not None:
                te.wait_ge(SS.h, ss_g[prev])
            for kb in range(KB):
                mm = te.matmul(pf[:, 0:NTOK], wq_sb[:, kb * C + ub * P:kb * C + (ub + 1) * P],
                               xqT[:, kb * NTOK:(kb + 1) * NTOK], start=(kb == 0), stop=(kb == KB - 1))
            st_ffn[(tt, ub)] = ST.inc(mm)
            sc.wait_ge(ST.h, st_ffn[(tt, ub)])
            sc.wait_ge(qSH.h, sq_ysc)
            if prev is not None:
                sc.wait_ge(qG[prev[1] % 2].h, sq_g[prev])
            ss_g[(tt, ub)] = SS.inc(sc.activation(gst[ub % 2][:], pf[:, 0:NTOK], AF.Silu, scale=ysc[:]))
            sy.wait_ge(SS.h, ss_g[(tt, ub)])
            sq_g[(tt, ub)] = qG[ub % 2].inc(sy.dma_start(
                out=AP(g_sp.tensor, ub * P * TLOC + tt * NTOK, [[TLOC, P], [1, NTOK]]),
                in_=gst[ub % 2][:]), 16)
    sq_g_all = (qG[0].v, qG[1].v)
    if "xn2" in dbg_o:
        qDBG.inc(sy.dma_start(out=dbg_o["xn2"], in_=xn2_sp), 16)
    if "gsp" in dbg_o:
        qDBG.inc(sy.dma_start(out=dbg_o["gsp"], in_=g_sp), 16)

    last_ffn = st_ffn[(T3 - 1, KB - 1)]
    sc.wait_ge(ST.h, last_ffn)
    sc.wait_ge(qG[0].h, sq_g_all[0])
    sc.wait_ge(qG[1].h, sq_g_all[1])
    aq_l2 = {}; st_f1 = {}; ss_ge = {}; st_f2 = {}; ss_m = {}; sq_m = {}
    sv_cmb = {}; sq_out = {}; sq_xa3 = {}; sq_x3 = {}; aq_mt = {}; aq_gt = {}
    te.wait_ge(SS.h, ss_g[(T3 - 1, KB - 1)])
    te.wait_ge(SV.h, sv_tev_all)
    te.wait_ge(SV.h, sv_b1p)
    def ln2t_load(tt):
        for kb in range(KB):
            if tt > 0:
                sc.wait_ge(ST.h, st_f1[(tt - 1, HB - 1)])
            aq_l2[(tt, kb)] = qL2T.inc(sc.dma_start(
                out=ln2T[:, kb * NTOK:(kb + 1) * NTOK],
                in_=ln2_sp[tt * NTOK:(tt + 1) * NTOK, kb * P:(kb + 1) * P], transpose=True), 16)
    def combine_tile(tt):
        for rr in range(NTOK // P):
            i = tt * (NTOK // P) + rr
            prev = i - 1
            if prev >= 0:
                sy.wait_ge(SV.h, sv_cmb[prev])
                sy.wait_ge(qOUT.h, sq_out[prev])
                sc.wait_ge(SV.h, sv_cmb[prev])
            if i == 0:
                for k in range(4):
                    sy.wait_ge(qSPA[k].h, qSPA[k].v)
            sq_xa3[i] = qXA.inc(sy.dma_start(out=xatr[:], in_=xattn_sp[i * P:(i + 1) * P, :]), 16)
            sq_x3[i] = qXR.inc(sy.dma_start(out=xre[:], in_=x_in[i * P:(i + 1) * P, :]), 16)
            sc.wait_ge(qM[0].h, qM[0].v)
            sc.wait_ge(qM[1].h, qM[1].v)
            aq_mt[i] = qMT.inc(sc.dma_start(out=mlpT[:], in_=mlp_sp[:, i * P:(i + 1) * P], transpose=True), 16)
            aq_gt[i] = qGT.inc(sc.dma_start(out=gT[:], in_=g_sp[:, i * P:(i + 1) * P], transpose=True), 16)
            ve.wait_ge(qXA.h, sq_xa3[i])
            ve.wait_ge(qXR.h, sq_x3[i])
            ve.wait_ge(qMT.h, aq_mt[i])
            ve.wait_ge(qGT.h, aq_gt[i])
            if prev >= 0:
                ve.wait_ge(qOUT.h, sq_out[prev])
            ve.tensor_add(outb[:], xatr[:], mlpT[:])
            ve.tensor_mul(outb[:], outb[:], gT[:])
            SV.inc(ve.tensor_add(outb[:], outb[:], xre[:]))
            sv_cmb[i] = vfence()
            sy.wait_ge(SV.h, sv_cmb[i])
            sq_out[i] = qOUT.inc(sy.dma_start(out=out_hbm[i * P:(i + 1) * P, :], in_=outb[:]), 16)
    ln2t_load(0)
    for tt in range(T3):
        te.wait_ge(qL2T.h, aq_l2[(tt, KB - 1)])
        if tt == 0:
            te.wait_ge(SV.h, sv_wscale)
            te.wait_ge(qW2T.h, aq_w2t[HB - 1])
        for mb in range(HB):
            if mb % 4 < 2:
                pfs = pst[mb % 2][:, 0:NTOK]
            else:
                pfs = psc[0][:, (mb % 4 - 2) * NTOK:(mb % 4 - 1) * NTOK]
            prev = (tt, mb - 4) if mb >= 4 else ((tt - 1, HB - 4 + mb) if tt > 0 else None)
            if prev is not None:
                te.wait_ge(SS.h, ss_ge[prev])
            for kb in range(KB):
                mm = te.matmul(pfs, fc1wT[:, kb * HD + mb * P:kb * HD + (mb + 1) * P],
                               ln2T[:, kb * NTOK:(kb + 1) * NTOK], start=(kb == 0), stop=(kb == KB - 1))
            st_f1[(tt, mb)] = ST.inc(mm)
            sc.wait_ge(ST.h, st_f1[(tt, mb)])
            if tt > 0:
                sc.wait_ge(ST.h, st_f2[(tt - 1, KB - 1)])
            ss_ge[(tt, mb)] = SS.inc(sc.activation(gbig[:, mb * NTOK:(mb + 1) * NTOK], pfs,
                                                   AF.Gelu, bias=b1p[:, mb:mb + 1]))
        if tt + 1 < T3:
            ln2t_load(tt + 1)
        if tt > 0:
            combine_tile(tt - 1)
        te.wait_ge(SS.h, ss_ge[(tt, HB - 1)])
        for cb in range(KB):
            if cb % 4 < 2:
                pgs = (psA if cb % 2 == 0 else psB)[:, 0:NTOK]
            else:
                pgs = psc[1][:, (cb % 4 - 2) * NTOK:(cb % 4 - 1) * NTOK]
            prev = (tt, cb - 4) if cb >= 4 else ((tt - 1, KB - 4 + cb) if tt > 0 else None)
            prevm = (tt, cb - 2) if cb >= 2 else ((tt - 1, KB - 2 + cb) if tt > 0 else None)
            if prev is not None:
                te.wait_ge(SS.h, ss_m[prev])
            for hb in range(HB):
                mm = te.matmul(pgs, fc2wT[:, hb * C + cb * P:hb * C + (cb + 1) * P],
                               gbig[:, hb * NTOK:(hb + 1) * NTOK], start=(hb == 0), stop=(hb == HB - 1))
            st_f2[(tt, cb)] = ST.inc(mm)
            sc.wait_ge(ST.h, st_f2[(tt, cb)])
            if prevm is not None:
                sc.wait_ge(qM[prevm[1] % 2].h, sq_m[prevm])
            ss_m[(tt, cb)] = SS.inc(sc.activation(mst[cb % 2][:], pgs, AF.Identity, bias=fc2b_pb[:, cb:cb + 1]))
            sy.wait_ge(SS.h, ss_m[(tt, cb)])
            sq_m[(tt, cb)] = qM[cb % 2].inc(sy.dma_start(
                out=AP(mlp_sp.tensor, cb * P * TLOC + tt * NTOK, [[TLOC, P], [1, NTOK]]),
                in_=mst[cb % 2][:]), 16)
    for tt in [T3 - 1]:
        combine_tile(tt)
    if "mlp" in dbg_o:
        sy.wait_ge(qM[0].h, qM[0].v)
        sy.wait_ge(qM[1].h, qM[1].v)
        qDBG.inc(sy.dma_start(out=dbg_o["mlp"], in_=mlp_sp), 16)

    for q in ALLQ:
        if q.v:
            sy.wait_ge(q.h, q.v)

    ctx.close()
    return nc


def make_in_maps(inputs):
    x = np.ascontiguousarray(np.asarray(inputs["L0_attn_x"], dtype=np.float32))
    def f32(k):
        return np.ascontiguousarray(np.asarray(inputs[k], dtype=np.float32))
    base = {
        "log_tau": f32("L0_attn_log_tau").reshape(NST, 1),
        "log_dt": f32("L0_attn_log_dt").reshape(1, 1),
        "mix": f32("L0_attn_mix"),
        "dvec": f32("L0_attn_D").reshape(1, C),
        "n1w": f32("L0_attn_norm1_w").reshape(1, C),
        "n1b": f32("L0_attn_norm1_b").reshape(1, C),
        "n2w": f32("L0_attn_norm2_w").reshape(1, C),
        "n2b": f32("L0_attn_norm2_b").reshape(1, C),
        "fc1w": f32("L0_attn_fc1_w"),
        "fc1b": f32("L0_attn_fc1_b").reshape(1, HD),
        "fc2w": f32("L0_attn_fc2_w"),
        "fc2b": f32("L0_attn_fc2_b").reshape(1, C),
        "ffnw": f32("L0_ffn_weight"),
        "lng": f32("L0_ffn_ln_gamma").reshape(1, C),
        "lnb": f32("L0_ffn_ln_beta").reshape(1, C),
        "iota128": np.arange(P, dtype=np.float32).reshape(1, P),
        "iota128p1": (np.arange(P, dtype=np.float32) + 1).reshape(1, P),
        "rev127": (127.0 - np.arange(P, dtype=np.float32)).reshape(P, 1),
        "j128": (128.0 * np.arange(NT + 1, dtype=np.float32)).reshape(1, NT + 1),
    }
    in_maps = []
    for core in range(8):
        b, hf = core // 2, core % 2
        m = dict(base)
        m["x_sh"] = np.ascontiguousarray(x[b, hf * TLOC:(hf + 1) * TLOC, :])
        m["emask"] = np.array([[1.0 if hf == 0 else 0.0]], dtype=np.float32)
        m["omask"] = np.array([[1.0 if hf == 1 else 0.0]], dtype=np.float32)
        in_maps.append(m)
    return in_maps


def assemble(results):
    out = np.empty((4, 2 * TLOC, C), dtype=np.float32)
    for core in range(8):
        b, hf = core // 2, core % 2
        out[b, hf * TLOC:(hf + 1) * TLOC, :] = results[core]["out_sh"]
    return out


if __name__ == "__main__":
    nc = build()
    print("built, instructions:", nc.n_instructions())


_NC_CACHE = {}


def kernel(**inputs):
    from concourse.bass_utils import run_bass_kernel_spmd
    if "nc" not in _NC_CACHE:
        _NC_CACHE["nc"] = build()
    nc = _NC_CACHE["nc"]
    in_maps = make_in_maps(inputs)
    res = run_bass_kernel_spmd(nc, in_maps, core_ids=list(range(8)))
    return assemble(res.results)
